# version 15 (final)
# baseline (speedup 1.0000x reference)
import numpy as np
import ml_dtypes
from contextlib import ExitStack
import concourse.bass as bass
import concourse.mybir as mybir
from concourse.bass_utils import run_bass_kernel_spmd
from concourse.alu_op_type import AluOpType as OP

F32 = mybir.dt.float32
BF16 = mybir.dt.bfloat16
AF = mybir.ActivationFunctionType
AX = mybir.AxisListType
BFNP = ml_dtypes.bfloat16

NCORES = 8
BATCH, SEQ, DM = 2, 4096, 2048
TOK = 1024
NI = DM // 128
NH = 16
HD = 128
SCALE = 1.0 / np.sqrt(HD).astype(np.float32)
I_HID = 1408
NJ = I_HID // 128
NEX = 18
NTB = TOK // 128
TBF = SEQ // 128
EXP_SHIFT = -12.0


class Sem:
    __slots__ = ("h", "v")

    def __init__(self, handle):
        self.h = handle
        self.v = 0


class Plan:
    ENGINES = ("tensor", "vector", "scalar", "gpsimd", "sync")

    def __init__(self):
        self.ops = {e: [] for e in self.ENGINES}

    def emit(self, eng, fn, inc=None, k=1):
        if inc is not None:
            inc.v += k
            self.ops[eng].append(lambda e, fn=fn, s=inc, k=k: fn(e).then_inc(s.h, k))
        else:
            self.ops[eng].append(lambda e, fn=fn: fn(e))

    def wait(self, eng, sem, v=None):
        v = sem.v if v is None else v
        if v > 0:
            self.ops[eng].append(lambda e, s=sem, v=v: e.wait_ge(s.h, v))

    def into(self, block):
        for eng in self.ENGINES:
            ops = self.ops[eng]
            if ops:
                getattr(block, eng)(lambda e, ops=ops: [f(e) for f in ops])


def build_program(debug=False):
    nc = bass.Bass("TRN2", target_bir_lowering=False)
    scratch_kind = "ExternalOutput" if debug else "Internal"

    xb = nc.dram_tensor("xb", [SEQ, DM], F32, kind="ExternalInput").ap()
    wq_img = nc.dram_tensor("wq_img", [NI, 128, DM], F32, kind="ExternalInput").ap()
    wk_img = nc.dram_tensor("wk_img", [NI, 128, DM], F32, kind="ExternalInput").ap()
    wvq_img = nc.dram_tensor("wvq_img", [4, 128, NI * 512], F32, kind="ExternalInput").ap()
    wo_img = nc.dram_tensor("wo_img", [4, 128, NH * 512], F32, kind="ExternalInput").ap()
    gw_img = nc.dram_tensor("gw_img", [128, NI * 16], F32, kind="ExternalInput").ap()
    w13_img = nc.dram_tensor("w13_img", [NEX, NJ, 128, 4096], F32, kind="ExternalInput").ap()
    w2_img = nc.dram_tensor("w2_img", [NEX, 4, 128, NJ * 512], F32, kind="ExternalInput").ap()
    eyeones = nc.dram_tensor("eyeones", [128, 256], F32, kind="ExternalInput").ap()
    consts2 = nc.dram_tensor("consts2", [128, 676], F32, kind="ExternalInput").ap()

    out = nc.dram_tensor("out", [TOK, DM], F32, kind="ExternalOutput").ap()

    k_dram = nc.dram_tensor("k_dram", [NH, 128, SEQ], BF16, kind=scratch_kind).ap()
    v_dram = nc.dram_tensor("v_dram", [NH, 128, SEQ], BF16, kind=scratch_kind).ap()
    q_dram = nc.dram_tensor("q_dram", [NH, 128, TOK], BF16, kind=scratch_kind).ap()
    xn_tm_dram = nc.dram_tensor("xn_tm_dram", [TOK, DM], BF16, kind=scratch_kind).ap()
    if debug:
        dbg_xn = nc.dram_tensor("dbg_xn", [128, NI * TOK], F32, kind="ExternalOutput").ap()
        dbg_comb = nc.dram_tensor("dbg_comb", [128, NTB * 16], F32, kind="ExternalOutput").ap()
        dbg_accC = nc.dram_tensor("dbg_accC", [128, NTB * DM], F32, kind="ExternalOutput").ap()
        dbg_oall = nc.dram_tensor("dbg_oall", [128, NH * TOK], F32, kind="ExternalOutput").ap()

    es_g = ExitStack()
    sb = lambda n, s, d: es_g.enter_context(nc.sbuf_tensor(n, s, d))
    eyeones_bf = sb("eyeones_bf", [128, 256], BF16)
    ones_f32r = sb("ones_f32r", [1, 128], F32)
    eye_f32 = sb("eye_f32", [128, 128], F32)
    c_eps = sb("c_eps", [128, 1], F32)
    c_neg12 = sb("c_neg12", [128, 1], F32)
    comb = sb("comb", [128, NTB * 16], F32)
    mask_all = sb("mask_all", [128, NTB * 16], BF16)
    mposm1 = sb("mposm1", [128, NTB * 20], F32)
    mT_sb = sb("mT_sb", [20, NTB * 128], F32)
    iotaS_sb = sb("iotaS_sb", [128, 512], F32)
    iotaCol_sb = sb("iotaCol_sb", [128, 4], F32)
    tri_bf = sb("tri_bf", [128, 128], BF16)

    sems = {k: Sem(es_g.enter_context(nc.semaphore(k))) for k in
            ("wd", "pe", "act", "dve", "ld", "st", "io")}
    s_wd, s_pe, s_act, s_dve = sems["wd"], sems["pe"], sems["act"], sems["dve"]
    s_ld, s_st, s_io = sems["ld"], sems["st"], sems["io"]

    p = Plan()

    def fence(eng, *ss):
        for s in ss:
            p.wait(eng, s)

    p.emit("gpsimd", lambda g: g.dma_start(eyeones_bf[:], eyeones), inc=s_wd, k=16)
    p.emit("gpsimd", lambda g: g.dma_start(ones_f32r[:], eyeones[0:1, 128:256]), inc=s_wd, k=16)
    p.emit("gpsimd", lambda g: g.dma_start(iotaS_sb[:], consts2[:, 0:512]), inc=s_wd, k=16)
    p.emit("gpsimd", lambda g: g.dma_start(iotaCol_sb[:], consts2[:, 512:516]), inc=s_wd, k=16)
    p.emit("gpsimd", lambda g: g.dma_start(tri_bf[:], consts2[:, 516:644]), inc=s_wd, k=16)
    p.emit("gpsimd", lambda g: g.dma_start(
        mposm1[:].rearrange("p (a c) -> p a c", c=20)[:, :, 16:20],
        consts2[:, 644:676].rearrange("p (a c) -> p a c", c=4)), inc=s_wd, k=16)
    p.emit("gpsimd", lambda g: g.dma_start(eye_f32[:], eyeones[:, 0:128]), inc=s_wd, k=16)
    p.emit("vector", lambda v: v.memset(c_eps[:], 1e-5), inc=s_dve)
    p.emit("vector", lambda v: v.memset(c_neg12[:], EXP_SHIFT), inc=s_dve)

    es_a = ExitStack()
    sba = lambda n, s, d: es_a.enter_context(nc.sbuf_tensor(n, s, d))
    xn_fm = sba("xn_fm", [128, NI * SEQ], BF16)
    gw_sb = sba("gw_sb", [128, NI * 16], BF16)
    stg = [sba(f"stg{i}", [128, 512], BF16) for i in range(8)]
    ps_pr = [es_a.enter_context(nc.psum_tensor(f"ps_pr{i}", [128, 512], F32)) for i in range(3)]
    ps_gt_full = es_a.enter_context(nc.psum_tensor("ps_gt", [128, 512], F32))
    ps_gt = ps_gt_full[:, 0:16]
    es_a0 = ExitStack()
    sba0 = lambda n, s, d: es_a0.enter_context(nc.sbuf_tensor(n, s, d))
    x_tile = [sba0(f"x_tile{i}", [128, DM], F32) for i in range(2)]
    bnst = [sba0(f"bnst{i}", [128, 24], F32) for i in range(2)]
    mv = [sba0(f"mv{i}", [128, 2], F32) for i in range(2)]
    std = [sba0(f"std{i}", [128, 1], F32) for i in range(2)]
    rstd = [sba0(f"rstd{i}", [128, 1], F32) for i in range(2)]
    negmr = [sba0(f"negmr{i}", [128, 1], F32) for i in range(2)]
    xn_tm = [sba0(f"xn_tm{i}", [128, DM], BF16) for i in range(2)]
    ps_tr = [es_a0.enter_context(nc.psum_tensor(f"ps_tr{i}", [128, 1024], BF16)) for i in range(2)]

    p.emit("gpsimd", lambda g: g.dma_start(gw_sb[:], gw_img), inc=s_wd, k=16)

    tr_free = [0, 0]
    xtile_free = [0, 0]
    xntm_free = [0, 0]
    xntm_st_free = [0, 0]
    ld_val = [0, 0]
    aggr_done = [0, 0]
    small_done = [0, 0]
    act_xn = [0, 0]
    for tb in range(TBF + 1):
        par = tb % 2
        if tb < TBF:
            p.wait("sync", s_act, xtile_free[par])
            p.emit("sync", lambda s, tb=tb, par=par: s.dma_start(
                x_tile[par][:], xb[tb * 128:(tb + 1) * 128, :]), inc=s_ld, k=16)
            ld_val[par] = s_ld.v
            p.wait("vector", s_ld, ld_val[par])
            for kk in range(4):
                p.emit("vector", lambda v, par=par, kk=kk: v.bn_stats(
                    bnst[par][:, kk * 6:(kk + 1) * 6],
                    x_tile[par][:, kk * 512:(kk + 1) * 512]),
                    inc=(s_dve if kk == 3 else None))
            p.wait("vector", s_dve, s_dve.v)
            p.emit("vector", lambda v, par=par: v.bn_aggr(mv[par][:], bnst[par][:]),
                   inc=s_dve)
            aggr_done[par] = s_dve.v
        if tb >= 1:
            pb = (tb - 1) % 2
            tbb = tb - 1
            p.wait("scalar", s_dve, aggr_done[pb])
            p.emit("scalar", lambda s, pb=pb: s.activation(
                std[pb][:], mv[pb][:, 1:2], AF.Sqrt, bias=c_eps[:]), inc=s_act)
            p.wait("vector", s_act, s_act.v)
            p.emit("vector", lambda v, pb=pb: v.reciprocal(rstd[pb][:], std[pb][:]),
                   inc=s_dve)
            p.wait("vector", s_dve, s_dve.v)
            p.emit("vector", lambda v, pb=pb: v.tensor_scalar(
                negmr[pb][:], mv[pb][:, 0:1], rstd[pb][:], -1.0, OP.mult, OP.mult),
                inc=s_dve)
            small_done[pb] = s_dve.v
            p.wait("scalar", s_dve, small_done[pb])
            p.wait("scalar", s_pe, xntm_free[pb])
            p.wait("scalar", s_st, xntm_st_free[pb])
            p.emit("scalar", lambda s, pb=pb: s.activation(
                xn_tm[pb][:], x_tile[pb][:], AF.Identity, bias=negmr[pb][:],
                scale=rstd[pb][:]), inc=s_act)
            xtile_free[pb] = s_act.v
            act_xn[pb] = s_act.v
            if tbb < NTB:
                p.wait("sync", s_act, act_xn[pb])
                p.emit("sync", lambda s, tbb=tbb, pb=pb: s.dma_start(
                    xn_tm_dram[tbb * 128:(tbb + 1) * 128, :], xn_tm[pb][:]),
                    inc=s_st, k=16)
                xntm_st_free[pb] = s_st.v
            p.wait("tensor", s_act, act_xn[pb])
            for ic in range(NI):
                tp = (tbb * NI + ic) % 2
                p.wait("tensor", s_act, tr_free[tp])
                p.emit("tensor", lambda t, pb=pb, ic=ic, tp=tp: t.transpose(
                    ps_tr[tp][:, 0:128], xn_tm[pb][:, ic * 128:(ic + 1) * 128],
                    eyeones_bf[:, 0:128]), inc=s_pe)
                p.wait("scalar", s_pe, s_pe.v)
                p.emit("scalar", lambda s, tbb=tbb, ic=ic, tp=tp: s.activation(
                    xn_fm[:, ic * SEQ + tbb * 128: ic * SEQ + (tbb + 1) * 128],
                    ps_tr[tp][:, 0:128], AF.Identity), inc=s_act)
                tr_free[tp] = s_act.v
            xntm_free[pb] = s_pe.v

    a0_act = s_act.v
    es_a0.close()
    es_aw = ExitStack()
    wblk = [es_aw.enter_context(nc.sbuf_tensor(f"wblk{i}", [128, DM], BF16))
            for i in range(2)]


    wblk_free = [0, 0]
    wblk_ready = {}
    stg_free = [0] * 8
    pspr_free = [0, 0, 0]
    nst = 0

    def dma_wblk(idx, src_ap_fn):
        par = idx % 2
        p.wait("gpsimd", s_pe, wblk_free[par])
        p.emit("gpsimd", lambda g, src_ap_fn=src_ap_fn, par=par: g.dma_start(
            wblk[par][:], src_ap_fn()), inc=s_wd, k=16)
        wblk_ready[idx] = s_wd.v

    def wsrc(i):
        if i < 16:
            return lambda i=i: wk_img[i]
        return lambda i=i: wq_img[i - 16]

    fence("gpsimd", s_act, s_pe)
    dma_wblk(0, wsrc(0))
    dma_wblk(1, wsrc(1))
    for wi in range(32):
        oc = wi % 16
        is_q = wi >= 16
        par = wi % 2
        nt = 2 if is_q else 8
        p.wait("tensor", s_wd, wblk_ready[wi])
        for tt in range(nt):
            pp = (wi * 8 + tt) % 2
            p.wait("tensor", s_act, pspr_free[pp])
            for ic in range(NI):
                p.emit("tensor", lambda t, par=par, ic=ic, tt=tt, pp=pp: t.matmul(
                    ps_pr[pp][:], wblk[par][:, ic * 128:(ic + 1) * 128],
                    xn_fm[:, ic * SEQ + tt * 512: ic * SEQ + (tt + 1) * 512],
                    start=(ic == 0), stop=(ic == NI - 1)),
                    inc=(s_pe if ic == NI - 1 else None))
            sg_i = nst % 8
            nst += 1
            p.wait("scalar", s_pe, s_pe.v)
            p.wait("scalar", s_st, stg_free[sg_i])
            p.emit("scalar", lambda s, pp=pp, sg_i=sg_i, is_q=is_q: s.activation(
                stg[sg_i][:], ps_pr[pp][:], AF.Identity,
                scale=(float(SCALE) if is_q else 1.0)), inc=s_act)
            pspr_free[pp] = s_act.v
            p.wait("sync", s_act, s_act.v)
            if is_q:
                p.emit("sync", lambda s, oc=oc, tt=tt, sg_i=sg_i: s.dma_start(
                    q_dram[oc, :, tt * 512:(tt + 1) * 512], stg[sg_i][:]), inc=s_st, k=16)
            else:
                p.emit("sync", lambda s, oc=oc, tt=tt, sg_i=sg_i: s.dma_start(
                    k_dram[oc, :, tt * 512:(tt + 1) * 512], stg[sg_i][:]), inc=s_st, k=16)
            stg_free[sg_i] = s_st.v
        wblk_free[par] = s_pe.v
        if wi + 2 < 32:
            dma_wblk(wi + 2, wsrc(wi + 2))

    es_aw.close()
    es_a1 = ExitStack()
    wvq_sb = [es_a1.enter_context(nc.sbuf_tensor(f"wvq_sb{i}", [128, NI * 512], BF16))
              for i in range(2)]
    wvq_free = [s_pe.v, s_pe.v]
    wvq_ready = {}

    def dma_wvq(q):
        p.wait("gpsimd", s_pe, wvq_free[q % 2])
        p.emit("gpsimd", lambda g, q=q: g.dma_start(wvq_sb[q % 2][:], wvq_img[q]),
               inc=s_wd, k=16)
        wvq_ready[q] = s_wd.v

    dma_wvq(0)
    dma_wvq(1)
    for q in range(4):
        p.wait("tensor", s_wd, wvq_ready[q])
        for tb in range(TBF):
            pp = (q * TBF + tb) % 3
            p.wait("tensor", s_act, pspr_free[pp])
            for ic in range(NI):
                p.emit("tensor", lambda t, q=q, tb=tb, ic=ic, pp=pp: t.matmul(
                    ps_pr[pp][:],
                    xn_fm[:, ic * SEQ + tb * 128: ic * SEQ + (tb + 1) * 128],
                    wvq_sb[q % 2][:, ic * 512:(ic + 1) * 512],
                    start=(ic == 0), stop=(ic == NI - 1)),
                    inc=(s_pe if ic == NI - 1 else None))
            wvq_free[q % 2] = s_pe.v
            sg_i = nst % 8
            nst += 1
            p.wait("scalar", s_pe, s_pe.v)
            p.wait("scalar", s_st, stg_free[sg_i])
            p.emit("scalar", lambda s, pp=pp, sg_i=sg_i: s.activation(
                stg[sg_i][:], ps_pr[pp][:], AF.Identity), inc=s_act)
            pspr_free[pp] = s_act.v
            p.wait("sync", s_act, s_act.v)
            p.emit("sync", lambda s, q=q, tb=tb, sg_i=sg_i: s.dma_start(
                v_dram[4 * q: 4 * q + 4, :, tb * 128:(tb + 1) * 128]
                .transpose([1, 0, 2]),
                stg[sg_i][:].rearrange("p (h j) -> p h j", h=4)), inc=s_st, k=16)
            stg_free[sg_i] = s_st.v
        if q + 2 < 4:
            dma_wvq(q + 2)

    e_sb = es_a1.enter_context(nc.sbuf_tensor("e_sb", [128, 16], F32))
    l_max = es_a1.enter_context(nc.sbuf_tensor("l_max", [128, 1], F32))
    zacc = es_a1.enter_context(nc.sbuf_tensor("zacc", [128, 1], F32))
    max8t = es_a1.enter_context(nc.sbuf_tensor("max8t", [128, 8], F32))
    s6 = es_a1.enter_context(nc.sbuf_tensor("s6", [128, 1], F32))
    den = es_a1.enter_context(nc.sbuf_tensor("den", [128, 1], F32))
    rden = es_a1.enter_context(nc.sbuf_tensor("rden", [128, 1], F32))
    tmp16a = es_a1.enter_context(nc.sbuf_tensor("tmp16a", [128, 16], F32))
    tmp16b = es_a1.enter_context(nc.sbuf_tensor("tmp16b", [128, 16], F32))
    for tb in range(NTB):
        p.wait("tensor", s_act, s_act.v)
        for ic in range(NI):
            p.emit("tensor", lambda t, tb=tb, ic=ic: t.matmul(
                ps_gt, xn_fm[:, ic * SEQ + tb * 128: ic * SEQ + (tb + 1) * 128],
                gw_sb[:, ic * 16:(ic + 1) * 16],
                start=(ic == 0), stop=(ic == NI - 1)),
                inc=(s_pe if ic == NI - 1 else None))
        p.wait("vector", s_pe, s_pe.v)
        p.emit("vector", lambda v: v.reduce_max(l_max[:], ps_gt, axis=AX.X,
                                                negate=True), inc=s_dve)
        p.wait("scalar", s_dve, s_dve.v)
        p.emit("scalar", lambda s: s.activation(e_sb[:], ps_gt, AF.Exp,
                                                bias=l_max[:], accum_out=zacc[:]),
               inc=s_act)
        p.wait("vector", s_act, s_act.v)
        p.emit("vector", lambda v: v.max(max8t[:], e_sb[:]), inc=s_dve)
        p.wait("vector", s_dve, s_dve.v)
        p.emit("vector", lambda v: v.reduce_sum(s6[:], max8t[:, 0:6], axis=AX.X),
               inc=s_dve)
        p.wait("vector", s_dve, s_dve.v)
        p.emit("vector", lambda v: v.tensor_scalar(den[:], zacc[:], 1e-20, s6[:],
                                                   OP.mult, OP.add), inc=s_dve)
        p.wait("vector", s_dve, s_dve.v)
        p.emit("vector", lambda v: v.reciprocal(rden[:], den[:]), inc=s_dve)
        p.wait("vector", s_dve, s_dve.v)
        p.emit("vector", lambda v, tb=tb: v.tensor_scalar(
            mask_all[:, tb * 16:(tb + 1) * 16], e_sb[:], max8t[:, 5:6],
            None, OP.is_ge), inc=s_dve)
        p.emit("vector", lambda v: v.tensor_scalar(e_sb[:], e_sb[:], rden[:],
                                                   None, OP.mult), inc=s_dve)
        p.wait("vector", s_dve, s_dve.v)
        p.emit("vector", lambda v, tb=tb: v.tensor_tensor(
            comb[:, tb * 16:(tb + 1) * 16], e_sb[:],
            mask_all[:, tb * 16:(tb + 1) * 16], OP.mult), inc=s_dve)
    if debug:
        p.wait("gpsimd", s_dve, s_dve.v)
        p.emit("gpsimd", lambda g: g.dma_start(dbg_comb, comb[:]), inc=s_st, k=16)

    mask_done = s_dve.v
    for tb in range(NTB):
        p.wait("tensor", s_dve, mask_done)
        p.wait("tensor", s_dve, s_dve.v)
        for tc in range(tb + 1):
            p.emit("tensor", lambda t, tb=tb, tc=tc: t.matmul(
                ps_gt, tri_bf[:] if tc == tb else eyeones_bf[:, 128:256],
                mask_all[:, tc * 16:(tc + 1) * 16],
                start=(tc == 0), stop=(tc == tb)),
                inc=(s_pe if tc == tb else None))
        p.wait("vector", s_pe, s_pe.v)
        p.emit("vector", lambda v: v.tensor_scalar(tmp16a[:], ps_gt, 1.0,
                                                   None, OP.add), inc=s_dve)
        p.wait("vector", s_dve, s_dve.v)
        p.emit("vector", lambda v, tb=tb: v.tensor_tensor(
            tmp16b[:], tmp16a[:], mask_all[:, tb * 16:(tb + 1) * 16], OP.mult),
            inc=s_dve)
        p.wait("vector", s_dve, s_dve.v)
        p.emit("vector", lambda v, tb=tb: v.tensor_scalar(
            mposm1[:, tb * 20: tb * 20 + 16], tmp16b[:], -2.0, None, OP.add),
            inc=s_dve)
    mpos_done = s_dve.v
    p.wait("tensor", s_dve, mpos_done)
    for tb in range(NTB):
        par = tb % 2
        p.wait("tensor", s_dve, s_dve.v if tb >= 2 else 0)
        p.emit("tensor", lambda t, tb=tb, par=par: t.transpose(
            ps_pr[par][0:20, 0:128], mposm1[:, tb * 20:(tb + 1) * 20],
            eye_f32[:]), inc=s_pe)
        p.wait("vector", s_pe, s_pe.v)
        p.emit("vector", lambda v, tb=tb, par=par: v.tensor_copy(
            mT_sb[:, tb * 128:(tb + 1) * 128], ps_pr[par][0:20, 0:128]), inc=s_dve)

    a_pe_final = s_pe.v
    a_act_final = s_act.v
    a_dve_final = s_dve.v
    a_st_final = s_st.v
    es_a1.close()
    es_a.close()

    CAP = 512
    NSC = CAP // 128
    EIDX = list(range(16)) + [16, 16, 17, 17]
    MCOL = list(range(16)) + [16, 17, 18, 19]
    NPE = 20
    es_c = ExitStack()
    sbc = lambda n, s, d: es_c.enter_context(nc.sbuf_tensor(n, s, d))
    acc = sbc("acc", [128, NTB * DM], F32)
    es_moe = ExitStack()
    sbm = lambda n, s, d: es_moe.enter_context(nc.sbuf_tensor(n, s, d))
    xn_tm_sb = sbm("xn_tm_sb", [128, NTB * DM], BF16)
    w13r = [sbm(f"w13r{i}", [128, 4096], BF16) for i in range(2)]
    w2r = [sbm(f"w2r{i}", [128, NJ * 512], BF16) for i in range(2)]
    sel_sb = sbm("sel_sb", [128, NTB * CAP], BF16)
    selT_sb = sbm("selT_sb", [128, NSC * TOK], BF16)
    xng_sb = sbm("xng_sb", [128, NI * CAP], BF16)
    hg_sb = sbm("hg_sb", [128, NJ * CAP], BF16)
    yg_b = [sbm(f"yg_b{i}", [128, NSC * 512], BF16) for i in range(2)]
    sg_b = [sbm(f"sg_b{i}", [128, CAP], F32) for i in range(2)]
    mrow_sb = sbm("mrow_sb", [1, TOK], F32)
    ps_g = es_moe.enter_context(nc.psum_tensor("ps_g", [128, 1024], F32))
    ps_u = es_moe.enter_context(nc.psum_tensor("ps_u", [128, 1024], F32))
    ps_y = [es_moe.enter_context(nc.psum_tensor(f"ps_y{i}", [128, 512], F32))
            for i in range(2)]
    ps_sc = [es_moe.enter_context(nc.psum_tensor(f"ps_sc{i}", [128, 512], F32))
             for i in range(2)]
    sc_slots = [ps_sc[0][:], ps_sc[1][:], None, None]

    sc_slots[2] = ps_g[:, 0:512]
    sc_slots[3] = ps_g[:, 512:1024]
    fence("gpsimd", s_pe, s_act, s_dve, s_st)
    fence("vector", s_st)
    fence("sync", s_pe, s_act, s_dve)
    p.emit("sync", lambda s: s.dma_start(
        xn_tm_sb[:].rearrange("p (a c) -> p a c", c=DM),
        xn_tm_dram.rearrange("(a p) c -> p a c", p=128)), inc=s_ld, k=16)
    xn_tm_ready = s_ld.v

    w13_free = [0, 0]
    w2_free = [0, 0]
    w13_ready = {}
    w2_ready = {}
    sg_free = [0, 0]
    psg_free = 0
    ga_free = [0, 0]
    psu_free = [0, 0]
    posb_ps_free = 0
    psy_free = [0, 0]
    sc_free = [0, 0, 0, 0]
    acc_ready = {}
    sel_free = 0
    mrow_free = 0
    selT_free = 0
    xng_free = 0
    hg_free = 0
    yg_free = [0, 0]
    gj = 0
    gq = 0

    def dma_w13(l, j):
        slot = (l * NJ + j) % 2
        p.wait("gpsimd", s_pe, w13_free[slot])
        p.emit("gpsimd", lambda g, l=l, j=j, slot=slot: g.dma_start(
            w13r[slot][:], w13_img[EIDX[l], j]), inc=s_wd, k=16)
        w13_ready[(l, j)] = s_wd.v

    def dma_w2(l, q):
        slot = (l * 4 + q) % 2
        p.wait("gpsimd", s_pe, w2_free[slot])
        p.emit("gpsimd", lambda g, l=l, q=q, slot=slot: g.dma_start(
            w2r[slot][:], w2_img[EIDX[l], q]), inc=s_wd, k=16)
        w2_ready[(l, q)] = s_wd.v

    for k in range(2):
        dma_w13(0, k)
    dma_w2(0, 0)
    dma_w2(0, 1)

    def emit_sel(l):
        mc = MCOL[l]
        p.wait("vector", s_pe, sel_free)
        for tb in range(NTB):
            p.emit("vector", lambda v, tb=tb, mc=mc: v.tensor_scalar(
                sel_sb[:, tb * CAP:(tb + 1) * CAP], iotaS_sb[:, 0:CAP],
                mposm1[:, tb * 20 + mc: tb * 20 + mc + 1], None, OP.is_equal),
                inc=s_dve)
        return s_dve.v

    sel_done = emit_sel(0)
    for l in range(NPE):
        mc = MCOL[l]
        p.wait("gpsimd", s_pe, mrow_free)
        p.emit("gpsimd", lambda g, mc=mc: g.dma_start(
            mrow_sb[:], mT_sb[mc:mc + 1, :]), inc=s_wd, k=16)
        mrow_ready = s_wd.v
        p.wait("tensor", s_dve, max(psu_free))
        p.wait("tensor", s_dve, posb_ps_free)
        p.wait("tensor", s_wd, mrow_ready)
        for half in range(2):
            p.emit("tensor", lambda t, half=half: t.matmul(
                ps_u[:, half * 512:(half + 1) * 512],
                ones_f32r[:],
                mrow_sb[:, half * 512:(half + 1) * 512],
                start=True, stop=True), inc=(s_pe if half == 1 else None))
        mrow_free = s_pe.v
        p.wait("vector", s_pe, s_pe.v)
        p.wait("vector", s_pe, selT_free)
        for sc in range(NSC):
            p.emit("vector", lambda v, sc=sc: v.tensor_scalar(
                selT_sb[:, sc * TOK:(sc + 1) * TOK], ps_u[:],
                iotaCol_sb[:, sc:sc + 1], None, OP.is_equal), inc=s_dve)
        selT_done = s_dve.v
        posb_ps_free = s_dve.v
        p.wait("tensor", s_dve, sel_done)
        if l == 0:
            p.wait("tensor", s_ld, xn_tm_ready)
        p.wait("tensor", s_pe, xng_free)
        for dc in range(NI):
            par = dc % 2
            p.wait("tensor", s_act, max(ga_free[par], psg_free))
            p.wait("tensor", s_dve, sc_free[2 + par])
            for tb in range(NTB):
                p.emit("tensor", lambda t, dc=dc, tb=tb, par=par: t.matmul(
                    ps_g[:, par * 512:(par + 1) * 512],
                    xn_tm_sb[:, tb * DM + dc * 128: tb * DM + (dc + 1) * 128],
                    sel_sb[:, tb * CAP:(tb + 1) * CAP],
                    start=(tb == 0), stop=(tb == NTB - 1)),
                    inc=(s_pe if tb == NTB - 1 else None))
            p.wait("scalar", s_pe, s_pe.v)
            p.emit("scalar", lambda s, dc=dc, par=par: s.activation(
                xng_sb[:, dc * CAP:(dc + 1) * CAP], ps_g[:, par * 512:(par + 1) * 512],
                AF.Identity), inc=s_act)
            ga_free[par] = s_act.v
        sel_free = s_pe.v
        ga_done = s_act.v
        for j in range(NJ):
            slot = (l * NJ + j) % 2
            par = gj % 2
            p.wait("tensor", s_wd, w13_ready[(l, j)])
            p.wait("tensor", s_act, max(psg_free, ga_done))
            for ic in range(NI):
                p.emit("tensor", lambda tt, slot=slot, ic=ic: tt.matmul(
                    ps_g[:, 0:512],
                    w13r[slot][:, ic * 128:(ic + 1) * 128],
                    xng_sb[:, ic * CAP:(ic + 1) * CAP],
                    start=(ic == 0), stop=(ic == NI - 1)),
                    inc=(s_pe if ic == NI - 1 else None))
            pe_g_done = s_pe.v
            uh = j % 2
            p.wait("tensor", s_dve, psu_free[uh])
            p.wait("tensor", s_dve, posb_ps_free)
            for ic in range(NI):
                p.emit("tensor", lambda tt, slot=slot, ic=ic, uh=uh: tt.matmul(
                    ps_u[:, uh * 512:(uh + 1) * 512],
                    w13r[slot][:, 2048 + ic * 128: 2048 + (ic + 1) * 128],
                    xng_sb[:, ic * CAP:(ic + 1) * CAP],
                    start=(ic == 0), stop=(ic == NI - 1)),
                    inc=(s_pe if ic == NI - 1 else None))
            pe_u_done = s_pe.v
            w13_free[slot] = s_pe.v
            p.wait("scalar", s_pe, pe_g_done)
            p.wait("scalar", s_dve, sg_free[par])
            p.emit("scalar", lambda s, par=par: s.activation(
                sg_b[par][:], ps_g[:, 0:512], AF.Silu), inc=s_act)
            psg_free = s_act.v
            p.wait("vector", s_act, s_act.v)
            p.wait("vector", s_pe, pe_u_done)
            if j == 0:
                p.wait("vector", s_pe, hg_free)
            p.emit("vector", lambda v, j=j, par=par, uh=uh: v.tensor_tensor(
                hg_sb[:, j * CAP:(j + 1) * CAP], sg_b[par][:],
                ps_u[:, uh * 512:(uh + 1) * 512],
                OP.mult), inc=s_dve)
            psu_free[uh] = s_dve.v
            sg_free[par] = s_dve.v
            gj += 1
            nxt = l * NJ + j + 2
            if nxt < NPE * NJ:
                dma_w13(nxt // NJ, nxt % NJ)
        h_done = s_dve.v
        xng_free = s_pe.v
        if l + 1 < NPE:
            sel_done = emit_sel(l + 1)
        p.wait("tensor", s_dve, selT_done)
        yg_q_done = {}

        def emit_scatter(l, q):
            qpar = (l * 4 + q) % 2
            p.wait("tensor", s_act, yg_q_done[q])
            for tb in range(NTB):
                par = tb % 4
                slot = sc_slots[par]
                p.wait("tensor", s_dve, sc_free[par])
                if par >= 2:
                    p.wait("tensor", s_act, psg_free)
                for sc in range(NSC):
                    p.emit("tensor", lambda t, tb=tb, sc=sc, slot=slot, qpar=qpar: t.matmul(
                        slot,
                        selT_sb[:, sc * TOK + tb * 128: sc * TOK + tb * 128 + 128],
                        yg_b[qpar][:, sc * 512:(sc + 1) * 512],
                        start=(sc == 0), stop=(sc == NSC - 1)),
                        inc=(s_pe if sc == NSC - 1 else None))
                p.wait("vector", s_pe, s_pe.v)
                if (tb, q) in acc_ready:
                    p.wait("vector", s_dve, acc_ready[(tb, q)])
                accsl = lambda tb=tb, q=q: acc[:, tb * DM + q * 512: tb * DM + (q + 1) * 512]
                if l == 0:
                    p.emit("vector", lambda v, tb=tb, q=q, slot=slot, l=l, accsl=accsl:
                           v.tensor_scalar(accsl(), slot,
                                           comb[:, tb * 16 + l: tb * 16 + l + 1],
                                           None, OP.mult), inc=s_dve)
                elif l < 16:
                    p.emit("vector", lambda v, tb=tb, q=q, slot=slot, l=l, accsl=accsl:
                           v.scalar_tensor_tensor(accsl(), slot,
                                                  comb[:, tb * 16 + l: tb * 16 + l + 1],
                                                  accsl(), OP.mult, OP.add), inc=s_dve)
                else:
                    p.emit("vector", lambda v, tb=tb, q=q, slot=slot, accsl=accsl:
                           v.tensor_tensor(accsl(), slot, accsl(), OP.add),
                           inc=s_dve)
                sc_free[par] = s_dve.v
                acc_ready[(tb, q)] = s_dve.v
            return s_pe.v

        for q in range(4):
            slot = (l * 4 + q) % 2
            qpar = (l * 4 + q) % 2
            p.wait("tensor", s_wd, w2_ready[(l, q)])
            p.wait("tensor", s_pe, yg_free[qpar])
            for sc in range(NSC):
                par = gq % 2
                p.wait("tensor", s_act, psy_free[par])
                if sc == 0 and q == 0:
                    p.wait("tensor", s_dve, h_done)
                for ic in range(NJ):
                    p.emit("tensor", lambda tt, slot=slot, ic=ic, sc=sc, par=par:
                           tt.matmul(ps_y[par][:],
                                     hg_sb[:, ic * CAP + sc * 128: ic * CAP + sc * 128 + 128],
                                     w2r[slot][:, ic * 512:(ic + 1) * 512],
                                     start=(ic == 0), stop=(ic == NJ - 1)),
                           inc=(s_pe if ic == NJ - 1 else None))
                p.wait("scalar", s_pe, s_pe.v)
                p.emit("scalar", lambda s, qpar=qpar, sc=sc, par=par: s.activation(
                    yg_b[qpar][:, sc * 512:(sc + 1) * 512],
                    ps_y[par][:], AF.Identity), inc=s_act)
                psy_free[par] = s_act.v
                gq += 1
            w2_free[slot] = s_pe.v
            yg_q_done[q] = s_act.v
            nxtq = l * 4 + q + 2
            if nxtq < NPE * 4:
                dma_w2(nxtq // 4, nxtq % 4)
            if q >= 1:
                yg_free[(l * 4 + q - 1) % 2] = emit_scatter(l, q - 1)
        yg_free[(l * 4 + 3) % 2] = emit_scatter(l, 3)
        hg_free = s_pe.v
        selT_free = s_pe.v

    c_pe_final = s_pe.v
    c_dve_final = s_dve.v
    c_act_final = s_act.v
    if debug:
        p.wait("gpsimd", s_dve, s_dve.v)
        p.emit("gpsimd", lambda g: g.dma_start(dbg_accC, acc[:]), inc=s_st, k=16)
    es_moe.close()

    es_b = ExitStack()
    sbb = lambda n, s, d: es_b.enter_context(nc.sbuf_tensor(n, s, d))
    o_all = sbb("o_all", [128, NH * TOK], BF16)
    wo_sb = [sbb(f"wo_sb{i}", [128, NH * 512], BF16) for i in range(2)]
    es_b1 = ExitStack()
    sbb1 = lambda n, s, d: es_b1.enter_context(nc.sbuf_tensor(n, s, d))
    k_sb = [sbb1(f"k_sb{i}", [128, SEQ], BF16) for i in range(2)]
    v_sb = [sbb1(f"v_sb{i}", [128, SEQ], BF16) for i in range(2)]
    q_sb = [sbb1(f"q_sb{i}", [128, TOK], BF16) for i in range(2)]
    p_sb = [sbb1(f"p_sb{i}", [128, 1024], BF16) for i in range(2)]
    rb_sb = [sbb1(f"rb_sb{i}", [128, 512], F32) for i in range(2)]
    ps_s = [es_b1.enter_context(nc.psum_tensor(f"ps_s{i}", [128, 1024], F32))
            for i in range(2)]
    ps_oz = [es_b1.enter_context(nc.psum_tensor(f"ps_oz{i}", [128, 1024], F32))
             for i in range(2)]

    fence("sync", s_pe, s_dve, s_act)
    fence("gpsimd", s_pe, s_dve)

    kvq_ready = {}
    kvq_free = [0, 0]

    def dma_head(hh):
        par = hh % 2
        p.wait("sync", s_pe, kvq_free[par])
        p.emit("sync", lambda s, hh=hh, par=par: s.dma_start(k_sb[par][:], k_dram[hh]),
               inc=s_ld, k=16)
        p.emit("sync", lambda s, hh=hh, par=par: s.dma_start(v_sb[par][:], v_dram[hh]),
               inc=s_ld, k=16)
        p.emit("sync", lambda s, hh=hh, par=par: s.dma_start(q_sb[par][:], q_dram[hh]),
               inc=s_ld, k=16)
        kvq_ready[hh] = s_ld.v

    dma_head(0)
    dma_head(1)
    wo_ready = {}
    wo_free = [0, 0]

    def dma_wo(q):
        slot = q % 2
        p.wait("gpsimd", s_pe, wo_free[slot])
        p.emit("gpsimd", lambda g, q=q, slot=slot: g.dma_start(
            wo_sb[slot][:], wo_img[q]), inc=s_wd, k=16)
        wo_ready[q] = s_wd.v

    dma_wo(0)
    dma_wo(1)

    pss_free = [0, 0]
    pso_free = 0
    psz_free = 0
    p_free = [0, 0]
    NKC = SEQ // 128
    NPC = NKC // 2

    def emit_pv_pair(par, pc, opar):
        pr = pc % 2
        for half in range(2):
            kc = 2 * pc + half
            p.emit("tensor", lambda t, par=par, kc=kc, pr=pr, half=half, opar=opar:
                   t.matmul(ps_oz[opar][:, 0:512],
                            v_sb[par][:, kc * 128:(kc + 1) * 128],
                            p_sb[pr][:, half * 512:(half + 1) * 512],
                            start=(kc == 0), stop=(kc == NKC - 1)))
            p.emit("tensor", lambda t, kc=kc, pr=pr, half=half, opar=opar: t.matmul(
                ps_oz[opar][:, 512:1024], eyeones_bf[:, 128:256],
                p_sb[pr][:, half * 512:(half + 1) * 512],
                start=(kc == 0), stop=(kc == NKC - 1)),
                inc=(s_pe if half == 1 else None))
        p_free[pr] = s_pe.v

    pso_free2 = [0, 0]
    for hh in range(NH):
        par = hh % 2
        p.wait("tensor", s_ld, kvq_ready[hh])
        for qt in range(2):
            opar = (hh * 2 + qt) % 2
            p.wait("tensor", s_dve, pso_free2[opar])
            exp_done = {}
            for pc in range(NPC):
                sp = pc % 2
                pr = pc % 2
                p.wait("tensor", s_act, pss_free[sp])
                for half in range(2):
                    kc = 2 * pc + half
                    p.emit("tensor", lambda t, par=par, qt=qt, kc=kc, sp=sp, half=half:
                           t.matmul(ps_s[sp][:, half * 512:(half + 1) * 512],
                                    k_sb[par][:, kc * 128:(kc + 1) * 128],
                                    q_sb[par][:, qt * 512:(qt + 1) * 512],
                                    start=True, stop=True),
                           inc=(s_pe if half == 1 else None))
                p.wait("scalar", s_pe, s_pe.v)
                p.wait("scalar", s_pe, p_free[pr])
                p.emit("scalar", lambda s, sp=sp, pr=pr: s.activation(
                    p_sb[pr][:], ps_s[sp][:], AF.Exp, bias=c_neg12[:]), inc=s_act)
                pss_free[sp] = s_act.v
                exp_done[pc] = s_act.v
                if pc >= 1:
                    p.wait("tensor", s_act, exp_done[pc - 1])
                    emit_pv_pair(par, pc - 1, opar)
            p.wait("tensor", s_act, exp_done[NPC - 1])
            emit_pv_pair(par, NPC - 1, opar)
            pv_done = s_pe.v
            p.wait("vector", s_pe, pv_done)
            p.emit("vector", lambda v, opar=opar: v.reciprocal(
                rb_sb[opar][:], ps_oz[opar][:, 512:1024]), inc=s_dve)
            p.wait("vector", s_dve, s_dve.v)
            p.emit("vector", lambda v, hh=hh, qt=qt, opar=opar: v.tensor_tensor(
                o_all[:, hh * TOK + qt * 512: hh * TOK + (qt + 1) * 512],
                ps_oz[opar][:, 0:512], rb_sb[opar][:], OP.mult), inc=s_dve)
            pso_free2[opar] = s_dve.v
        kvq_free[par] = s_pe.v
        if hh + 2 < NH:
            dma_head(hh + 2)

    b1_pe = s_pe.v
    b1_dve = s_dve.v
    if debug:
        p.wait("gpsimd", s_dve, s_dve.v)
        p.emit("gpsimd", lambda g: g.dma_start(dbg_oall, o_all[:]), inc=s_st, k=16)
    es_b1.close()

    es_b2 = ExitStack()
    ps_w = [es_b2.enter_context(nc.psum_tensor(f"ps_w{i}", [128, 512], F32))
            for i in range(2)]
    psw_free = [0, 0]
    p.wait("tensor", s_dve, b1_dve)
    p.wait("tensor", s_act, s_act.v)
    for q in range(4):
        slot = q % 2
        p.wait("tensor", s_wd, wo_ready[q])
        for tb in range(NTB):
            par = (q * NTB + tb) % 2
            p.wait("tensor", s_dve, psw_free[par])
            for hh in range(NH):
                p.emit("tensor", lambda t, q=q, slot=slot, tb=tb, hh=hh, par=par:
                       t.matmul(ps_w[par][:, 0:512],
                                o_all[:, hh * TOK + tb * 128: hh * TOK + tb * 128 + 128],
                                wo_sb[slot][:, hh * 512:(hh + 1) * 512],
                                start=(hh == 0), stop=(hh == NH - 1)),
                       inc=(s_pe if hh == NH - 1 else None))
            p.wait("vector", s_pe, s_pe.v)
            p.wait("vector", s_dve, acc_ready[(tb, q)])
            p.emit("vector", lambda v, q=q, tb=tb, par=par: v.scalar_tensor_tensor(
                acc[:, tb * DM + q * 512: tb * DM + (q + 1) * 512],
                ps_w[par][:, 0:512], 1.0,
                acc[:, tb * DM + q * 512: tb * DM + (q + 1) * 512],
                OP.mult, OP.add), inc=s_dve)
            psw_free[par] = s_dve.v
            acc_ready[(tb, q)] = s_dve.v
        wo_free[slot] = s_pe.v
        if q + 2 < 4:
            dma_wo(q + 2)
    es_b2.close()
    es_b.close()

    es_d = ExitStack()
    xo_t = [es_d.enter_context(nc.sbuf_tensor(f"xo_t{i}", [128, DM], F32))
            for i in range(2)]
    o_t = [es_d.enter_context(nc.sbuf_tensor(f"o_t{i}", [128, DM], F32))
           for i in range(2)]
    fence("sync", s_pe, s_dve)
    ot_free = [0, 0]
    xo_free = [0, 0]
    for tb in range(NTB):
        par = tb % 2
        p.wait("sync", s_dve, xo_free[par])
        p.emit("sync", lambda s, tb=tb, par=par: s.dma_start(
            xo_t[par][:], xb[tb * 128:(tb + 1) * 128, :]), inc=s_ld, k=16)
        p.wait("vector", s_ld, s_ld.v)
        p.wait("vector", s_dve, max(acc_ready[(tb, q)] for q in range(4)))
        p.wait("vector", s_st, ot_free[par])
        p.emit("vector", lambda v, tb=tb, par=par: v.tensor_tensor(
            o_t[par][:], acc[:, tb * DM:(tb + 1) * DM], xo_t[par][:], OP.add),
            inc=s_dve)
        xo_free[par] = s_dve.v
        p.wait("sync", s_dve, s_dve.v)
        p.emit("sync", lambda s, tb=tb, par=par: s.dma_start(
            out[tb * 128:(tb + 1) * 128, :], o_t[par][:]), inc=s_st, k=16)
        ot_free[par] = s_st.v
    p.wait("sync", s_st, s_st.v)
    p.wait("gpsimd", s_st, s_st.v)

    es_d.close()
    es_c.close()

    with nc.Block() as blk:
        p.into(blk)
    es_g.close()
    n_inst = sum(len(v) for v in p.ops.values())
    return nc, n_inst


def _images(L0_attn_wq, L0_attn_wk, L0_attn_wv, L0_attn_wo,
            L0_ffn_gate_w, L0_ffn_w13, L0_ffn_w2, L0_ffn_sw13, L0_ffn_sw2):
    f32 = np.float32
    wq = np.asarray(L0_attn_wq, f32)
    wk = np.asarray(L0_attn_wk, f32)
    wv = np.asarray(L0_attn_wv, f32)
    wo = np.asarray(L0_attn_wo, f32)
    gw = np.asarray(L0_ffn_gate_w, f32)
    w13 = np.asarray(L0_ffn_w13, f32)
    w2 = np.asarray(L0_ffn_w2, f32)
    sw13 = np.asarray(L0_ffn_sw13, f32)
    sw2 = np.asarray(L0_ffn_sw2, f32)

    def colblocks(w):
        return np.ascontiguousarray(
            w.reshape(NI, 128, NI, 128).transpose(0, 3, 2, 1).reshape(NI, 128, DM))
    wq_img = colblocks(wq)
    wk_img = colblocks(wk)
    wvq_img = np.ascontiguousarray(
        wv.reshape(4, 512, NI, 128).transpose(0, 3, 2, 1).reshape(4, 128, NI * 512))
    wo_img = np.ascontiguousarray(
        wo.reshape(4, 512, NH, 128).transpose(0, 3, 2, 1).reshape(4, 128, NH * 512))
    gw_img = np.ascontiguousarray(
        gw.T.reshape(NI, 128, 16).transpose(1, 0, 2).reshape(128, NI * 16))
    w13v = w13.reshape(16, 2, NJ, 128, NI, 128)
    w13_r = w13v.transpose(0, 2, 5, 1, 4, 3).reshape(16, NJ, 128, 4096)
    sw13v = sw13.reshape(2, 2, NJ, 128, NI, 128)
    sw13_r = sw13v.transpose(1, 2, 5, 0, 4, 3).reshape(2, NJ, 128, 4096)
    w13_img = np.ascontiguousarray(np.concatenate([w13_r, sw13_r], axis=0))
    w2v = w2.reshape(16, 4, 512, NJ, 128)
    w2_r = w2v.transpose(0, 1, 4, 3, 2).reshape(16, 4, 128, NJ * 512)
    sw2v = sw2.reshape(4, 512, 2, NJ, 128)
    sw2_r = sw2v.transpose(2, 0, 4, 3, 1).reshape(2, 4, 128, NJ * 512)
    w2_img = np.ascontiguousarray(np.concatenate([w2_r, sw2_r], axis=0))

    eyeones = np.concatenate(
        [np.eye(128, dtype=f32), np.ones((128, 128), f32)], axis=1)
    iotaS = np.broadcast_to(np.arange(512, dtype=f32)[None, :], (128, 512))
    pcol = np.arange(128, dtype=f32)[:, None]
    iotaCol = pcol + 512.0 * 0 + np.arange(4, dtype=f32)[None, :] * 128.0
    tri = (np.arange(128)[:, None] <= np.arange(128)[None, :]).astype(f32)
    mconst = np.full((128, 8, 4), -2.0, f32)
    for tb in range(8):
        for q in range(4):
            first = (q % 2 == 0)
            if first and tb < 4:
                mconst[:, tb, q] = tb * 128 + pcol[:, 0]
            if (not first) and tb >= 4:
                mconst[:, tb, q] = (tb - 4) * 128 + pcol[:, 0]
    consts2 = np.ascontiguousarray(np.concatenate(
        [iotaS, iotaCol, tri, mconst.reshape(128, 32)], axis=1))

    return {"wq_img": wq_img, "wk_img": wk_img, "wvq_img": wvq_img,
            "wo_img": wo_img, "gw_img": gw_img, "w13_img": w13_img,
            "w2_img": w2_img, "eyeones": eyeones, "consts2": consts2}


def kernel(L0_attn_x, L0_attn_wq, L0_attn_wk, L0_attn_wv, L0_attn_wo,
           L0_ffn_gate_w, L0_ffn_w13, L0_ffn_w2, L0_ffn_sw13, L0_ffn_sw2):
    x = np.asarray(L0_attn_x, np.float32)
    imgs = _images(L0_attn_wq, L0_attn_wk, L0_attn_wv, L0_attn_wo,
                   L0_ffn_gate_w, L0_ffn_w13, L0_ffn_w2, L0_ffn_sw13, L0_ffn_sw2)
    nc, _ = build_program()
    in_maps = []
    for c in range(NCORES):
        b, qs = c // 4, c % 4
        xbat = x[b]
        own = xbat[qs * TOK:(qs + 1) * TOK]
        rest = np.concatenate([xbat[:qs * TOK], xbat[(qs + 1) * TOK:]], axis=0)
        xb_c = np.ascontiguousarray(np.concatenate([own, rest], axis=0))
        in_maps.append({"xb": xb_c, **imgs})
    res = run_bass_kernel_spmd(nc, in_maps, core_ids=list(range(NCORES)))
    outp = np.empty((BATCH, SEQ, DM), np.float32)
    for c in range(NCORES):
        b, qs = c // 4, c % 4
        outp[b, qs * TOK:(qs + 1) * TOK] = res.results[c]["out"]
    return outp


# version 17
# speedup vs baseline: 1.0003x; 1.0003x over previous
import numpy as np
import ml_dtypes
from contextlib import ExitStack
import concourse.bass as bass
import concourse.mybir as mybir
from concourse.bass_utils import run_bass_kernel_spmd
from concourse.alu_op_type import AluOpType as OP

F32 = mybir.dt.float32
BF16 = mybir.dt.bfloat16
AF = mybir.ActivationFunctionType
AX = mybir.AxisListType
BFNP = ml_dtypes.bfloat16

NCORES = 8
BATCH, SEQ, DM = 2, 4096, 2048
TOK = 1024
NI = DM // 128
NH = 16
HD = 128
SCALE = 1.0 / np.sqrt(HD).astype(np.float32)
I_HID = 1408
NJ = I_HID // 128
NEX = 18
NTB = TOK // 128
TBF = SEQ // 128
EXP_SHIFT = -12.0


class Sem:
    __slots__ = ("h", "v")

    def __init__(self, handle):
        self.h = handle
        self.v = 0


class Plan:
    ENGINES = ("tensor", "vector", "scalar", "gpsimd", "sync")

    def __init__(self):
        self.ops = {e: [] for e in self.ENGINES}

    def emit(self, eng, fn, inc=None, k=1):
        if inc is not None:
            inc.v += k
            self.ops[eng].append(lambda e, fn=fn, s=inc, k=k: fn(e).then_inc(s.h, k))
        else:
            self.ops[eng].append(lambda e, fn=fn: fn(e))

    def wait(self, eng, sem, v=None):
        v = sem.v if v is None else v
        if v > 0:
            self.ops[eng].append(lambda e, s=sem, v=v: e.wait_ge(s.h, v))

    def into(self, block):
        for eng in self.ENGINES:
            ops = self.ops[eng]
            if ops:
                getattr(block, eng)(lambda e, ops=ops: [f(e) for f in ops])


def build_program(debug=False):
    nc = bass.Bass("TRN2", target_bir_lowering=False)
    scratch_kind = "ExternalOutput" if debug else "Internal"

    xb = nc.dram_tensor("xb", [SEQ, DM], F32, kind="ExternalInput").ap()
    wq_img = nc.dram_tensor("wq_img", [NI, 128, DM], F32, kind="ExternalInput").ap()
    wk_img = nc.dram_tensor("wk_img", [NI, 128, DM], F32, kind="ExternalInput").ap()
    wvq_img = nc.dram_tensor("wvq_img", [4, 128, NI * 512], F32, kind="ExternalInput").ap()
    wo_img = nc.dram_tensor("wo_img", [4, 128, NH * 512], F32, kind="ExternalInput").ap()
    gw_img = nc.dram_tensor("gw_img", [128, NI * 16], F32, kind="ExternalInput").ap()
    w13_img = nc.dram_tensor("w13_img", [NEX, NJ, 128, 4096], F32, kind="ExternalInput").ap()
    w2_img = nc.dram_tensor("w2_img", [NEX, 4, 128, NJ * 512], F32, kind="ExternalInput").ap()
    eyeones = nc.dram_tensor("eyeones", [128, 256], F32, kind="ExternalInput").ap()
    consts2 = nc.dram_tensor("consts2", [128, 676], F32, kind="ExternalInput").ap()

    out = nc.dram_tensor("out", [TOK, DM], F32, kind="ExternalOutput").ap()

    k_dram = nc.dram_tensor("k_dram", [NH, 128, SEQ], BF16, kind=scratch_kind).ap()
    v_dram = nc.dram_tensor("v_dram", [NH, 128, SEQ], BF16, kind=scratch_kind).ap()
    q_dram = nc.dram_tensor("q_dram", [NH, 128, TOK], BF16, kind=scratch_kind).ap()
    xn_tm_dram = nc.dram_tensor("xn_tm_dram", [TOK, DM], BF16, kind=scratch_kind).ap()
    if debug:
        dbg_xn = nc.dram_tensor("dbg_xn", [128, NI * TOK], F32, kind="ExternalOutput").ap()
        dbg_comb = nc.dram_tensor("dbg_comb", [128, NTB * 16], F32, kind="ExternalOutput").ap()
        dbg_accC = nc.dram_tensor("dbg_accC", [128, NTB * DM], F32, kind="ExternalOutput").ap()
        dbg_oall = nc.dram_tensor("dbg_oall", [128, NH * TOK], F32, kind="ExternalOutput").ap()

    es_g = ExitStack()
    sb = lambda n, s, d: es_g.enter_context(nc.sbuf_tensor(n, s, d))
    eyeones_bf = sb("eyeones_bf", [128, 256], BF16)
    ones_f32r = sb("ones_f32r", [1, 128], F32)
    eye_f32 = sb("eye_f32", [128, 128], F32)
    c_eps = sb("c_eps", [128, 1], F32)
    c_neg12 = sb("c_neg12", [128, 1], F32)
    comb = sb("comb", [128, NTB * 16], F32)
    mask_all = sb("mask_all", [128, NTB * 16], BF16)
    mposm1 = sb("mposm1", [128, NTB * 20], F32)
    mT_sb = sb("mT_sb", [20, NTB * 128], F32)
    iotaS_sb = sb("iotaS_sb", [128, 512], F32)
    iotaCol_sb = sb("iotaCol_sb", [128, 4], F32)
    tri_bf = sb("tri_bf", [128, 128], BF16)

    sems = {k: Sem(es_g.enter_context(nc.semaphore(k))) for k in
            ("wd", "pe", "act", "dve", "ld", "st", "io")}
    s_wd, s_pe, s_act, s_dve = sems["wd"], sems["pe"], sems["act"], sems["dve"]
    s_ld, s_st, s_io = sems["ld"], sems["st"], sems["io"]

    p = Plan()

    def fence(eng, *ss):
        for s in ss:
            p.wait(eng, s)

    p.emit("gpsimd", lambda g: g.dma_start(eyeones_bf[:], eyeones), inc=s_wd, k=16)
    p.emit("gpsimd", lambda g: g.dma_start(ones_f32r[:], eyeones[0:1, 128:256]), inc=s_wd, k=16)
    p.emit("gpsimd", lambda g: g.dma_start(iotaS_sb[:], consts2[:, 0:512]), inc=s_wd, k=16)
    p.emit("gpsimd", lambda g: g.dma_start(iotaCol_sb[:], consts2[:, 512:516]), inc=s_wd, k=16)
    p.emit("gpsimd", lambda g: g.dma_start(tri_bf[:], consts2[:, 516:644]), inc=s_wd, k=16)
    p.emit("gpsimd", lambda g: g.dma_start(
        mposm1[:].rearrange("p (a c) -> p a c", c=20)[:, :, 16:20],
        consts2[:, 644:676].rearrange("p (a c) -> p a c", c=4)), inc=s_wd, k=16)
    p.emit("gpsimd", lambda g: g.dma_start(eye_f32[:], eyeones[:, 0:128]), inc=s_wd, k=16)
    p.emit("vector", lambda v: v.memset(c_eps[:], 1e-5), inc=s_dve)
    p.emit("vector", lambda v: v.memset(c_neg12[:], EXP_SHIFT), inc=s_dve)

    es_a = ExitStack()
    sba = lambda n, s, d: es_a.enter_context(nc.sbuf_tensor(n, s, d))
    xn_fm = sba("xn_fm", [128, NI * SEQ], BF16)
    gw_sb = sba("gw_sb", [128, NI * 16], BF16)
    stg = [sba(f"stg{i}", [128, 512], BF16) for i in range(8)]
    ps_pr = [es_a.enter_context(nc.psum_tensor(f"ps_pr{i}", [128, 512], F32)) for i in range(3)]
    ps_gt_full = es_a.enter_context(nc.psum_tensor("ps_gt", [128, 512], F32))
    ps_gt = ps_gt_full[:, 0:16]
    es_a0 = ExitStack()
    sba0 = lambda n, s, d: es_a0.enter_context(nc.sbuf_tensor(n, s, d))
    x_tile = [sba0(f"x_tile{i}", [128, DM], F32) for i in range(2)]
    bnst = [sba0(f"bnst{i}", [128, 24], F32) for i in range(2)]
    mv = [sba0(f"mv{i}", [128, 2], F32) for i in range(2)]
    std = [sba0(f"std{i}", [128, 1], F32) for i in range(2)]
    rstd = [sba0(f"rstd{i}", [128, 1], F32) for i in range(2)]
    negmr = [sba0(f"negmr{i}", [128, 1], F32) for i in range(2)]
    xn_tm = [sba0(f"xn_tm{i}", [128, DM], BF16) for i in range(2)]
    ps_tr = [es_a0.enter_context(nc.psum_tensor(f"ps_tr{i}", [128, 1024], BF16)) for i in range(2)]

    p.emit("gpsimd", lambda g: g.dma_start(gw_sb[:], gw_img), inc=s_wd, k=16)

    tr_free = [0, 0]
    xtile_free = [0, 0]
    xntm_free = [0, 0]
    xntm_st_free = [0, 0]
    ld_val = [0, 0]
    aggr_done = [0, 0]
    small_done = [0, 0]
    act_xn = [0, 0]
    for tb in range(TBF + 1):
        par = tb % 2
        if tb < TBF:
            p.wait("sync", s_act, xtile_free[par])
            p.emit("sync", lambda s, tb=tb, par=par: s.dma_start(
                x_tile[par][:], xb[tb * 128:(tb + 1) * 128, :]), inc=s_ld, k=16)
            ld_val[par] = s_ld.v
            p.wait("vector", s_ld, ld_val[par])
            for kk in range(4):
                p.emit("vector", lambda v, par=par, kk=kk: v.bn_stats(
                    bnst[par][:, kk * 6:(kk + 1) * 6],
                    x_tile[par][:, kk * 512:(kk + 1) * 512]),
                    inc=(s_dve if kk == 3 else None))
            p.wait("vector", s_dve, s_dve.v)
            p.emit("vector", lambda v, par=par: v.bn_aggr(mv[par][:], bnst[par][:]),
                   inc=s_dve)
            aggr_done[par] = s_dve.v
        if tb >= 1:
            pb = (tb - 1) % 2
            tbb = tb - 1
            p.wait("scalar", s_dve, aggr_done[pb])
            p.emit("scalar", lambda s, pb=pb: s.activation(
                std[pb][:], mv[pb][:, 1:2], AF.Sqrt, bias=c_eps[:]), inc=s_act)
            p.wait("vector", s_act, s_act.v)
            p.emit("vector", lambda v, pb=pb: v.reciprocal(rstd[pb][:], std[pb][:]),
                   inc=s_dve)
            p.wait("vector", s_dve, s_dve.v)
            p.emit("vector", lambda v, pb=pb: v.tensor_scalar(
                negmr[pb][:], mv[pb][:, 0:1], rstd[pb][:], -1.0, OP.mult, OP.mult),
                inc=s_dve)
            small_done[pb] = s_dve.v
            p.wait("scalar", s_dve, small_done[pb])
            p.wait("scalar", s_pe, xntm_free[pb])
            p.wait("scalar", s_st, xntm_st_free[pb])
            p.emit("scalar", lambda s, pb=pb: s.activation(
                xn_tm[pb][:], x_tile[pb][:], AF.Identity, bias=negmr[pb][:],
                scale=rstd[pb][:]), inc=s_act)
            xtile_free[pb] = s_act.v
            act_xn[pb] = s_act.v
            if tbb < NTB:
                p.wait("sync", s_act, act_xn[pb])
                p.emit("sync", lambda s, tbb=tbb, pb=pb: s.dma_start(
                    xn_tm_dram[tbb * 128:(tbb + 1) * 128, :], xn_tm[pb][:]),
                    inc=s_st, k=16)
                xntm_st_free[pb] = s_st.v
            p.wait("tensor", s_act, act_xn[pb])
            for ic in range(NI):
                tp = (tbb * NI + ic) % 2
                p.wait("tensor", s_act, tr_free[tp])
                p.emit("tensor", lambda t, pb=pb, ic=ic, tp=tp: t.transpose(
                    ps_tr[tp][:, 0:128], xn_tm[pb][:, ic * 128:(ic + 1) * 128],
                    eyeones_bf[:, 0:128]), inc=s_pe)
                p.wait("scalar", s_pe, s_pe.v)
                p.emit("scalar", lambda s, tbb=tbb, ic=ic, tp=tp: s.activation(
                    xn_fm[:, ic * SEQ + tbb * 128: ic * SEQ + (tbb + 1) * 128],
                    ps_tr[tp][:, 0:128], AF.Identity), inc=s_act)
                tr_free[tp] = s_act.v
            xntm_free[pb] = s_pe.v

    a0_act = s_act.v
    es_a0.close()
    es_aw = ExitStack()
    wblk = [es_aw.enter_context(nc.sbuf_tensor(f"wblk{i}", [128, DM], BF16))
            for i in range(2)]


    wblk_free = [0, 0]
    wblk_ready = {}
    stg_free = [0] * 8
    pspr_free = [0, 0, 0]
    nst = 0

    def dma_wblk(idx, src_ap_fn):
        par = idx % 2
        p.wait("gpsimd", s_pe, wblk_free[par])
        p.emit("gpsimd", lambda g, src_ap_fn=src_ap_fn, par=par: g.dma_start(
            wblk[par][:], src_ap_fn()), inc=s_wd, k=16)
        wblk_ready[idx] = s_wd.v

    def wsrc(i):
        if i < 16:
            return lambda i=i: wk_img[i]
        return lambda i=i: wq_img[i - 16]

    fence("gpsimd", s_act, s_pe)
    dma_wblk(0, wsrc(0))
    dma_wblk(1, wsrc(1))
    for wi in range(32):
        oc = wi % 16
        is_q = wi >= 16
        par = wi % 2
        nt = 2 if is_q else 8
        p.wait("tensor", s_wd, wblk_ready[wi])
        for tt in range(nt):
            pp = (wi * 8 + tt) % 2
            p.wait("tensor", s_act, pspr_free[pp])
            for ic in range(NI):
                p.emit("tensor", lambda t, par=par, ic=ic, tt=tt, pp=pp: t.matmul(
                    ps_pr[pp][:], wblk[par][:, ic * 128:(ic + 1) * 128],
                    xn_fm[:, ic * SEQ + tt * 512: ic * SEQ + (tt + 1) * 512],
                    start=(ic == 0), stop=(ic == NI - 1)),
                    inc=(s_pe if ic == NI - 1 else None))
            sg_i = nst % 8
            nst += 1
            p.wait("scalar", s_pe, s_pe.v)
            p.wait("scalar", s_st, stg_free[sg_i])
            p.emit("scalar", lambda s, pp=pp, sg_i=sg_i, is_q=is_q: s.activation(
                stg[sg_i][:], ps_pr[pp][:], AF.Identity,
                scale=(float(SCALE) if is_q else 1.0)), inc=s_act)
            pspr_free[pp] = s_act.v
            p.wait("scalar", s_act, s_act.v)
            if is_q:
                p.emit("scalar", lambda s, oc=oc, tt=tt, sg_i=sg_i: s.dma_start(
                    q_dram[oc, :, tt * 512:(tt + 1) * 512], stg[sg_i][:]), inc=s_st, k=16)
            else:
                p.emit("scalar", lambda s, oc=oc, tt=tt, sg_i=sg_i: s.dma_start(
                    k_dram[oc, :, tt * 512:(tt + 1) * 512], stg[sg_i][:]), inc=s_st, k=16)
            stg_free[sg_i] = s_st.v
        wblk_free[par] = s_pe.v
        if wi + 2 < 32:
            dma_wblk(wi + 2, wsrc(wi + 2))

    es_aw.close()
    es_a1 = ExitStack()
    wvq_sb = [es_a1.enter_context(nc.sbuf_tensor(f"wvq_sb{i}", [128, NI * 512], BF16))
              for i in range(2)]
    wvq_free = [s_pe.v, s_pe.v]
    wvq_ready = {}

    def dma_wvq(q):
        p.wait("gpsimd", s_pe, wvq_free[q % 2])
        p.emit("gpsimd", lambda g, q=q: g.dma_start(wvq_sb[q % 2][:], wvq_img[q]),
               inc=s_wd, k=16)
        wvq_ready[q] = s_wd.v

    dma_wvq(0)
    dma_wvq(1)
    for q in range(4):
        p.wait("tensor", s_wd, wvq_ready[q])
        for tb in range(TBF):
            pp = (q * TBF + tb) % 3
            p.wait("tensor", s_act, pspr_free[pp])
            for ic in range(NI):
                p.emit("tensor", lambda t, q=q, tb=tb, ic=ic, pp=pp: t.matmul(
                    ps_pr[pp][:],
                    xn_fm[:, ic * SEQ + tb * 128: ic * SEQ + (tb + 1) * 128],
                    wvq_sb[q % 2][:, ic * 512:(ic + 1) * 512],
                    start=(ic == 0), stop=(ic == NI - 1)),
                    inc=(s_pe if ic == NI - 1 else None))
            wvq_free[q % 2] = s_pe.v
            sg_i = nst % 8
            nst += 1
            p.wait("scalar", s_pe, s_pe.v)
            p.wait("scalar", s_st, stg_free[sg_i])
            p.emit("scalar", lambda s, pp=pp, sg_i=sg_i: s.activation(
                stg[sg_i][:], ps_pr[pp][:], AF.Identity), inc=s_act)
            pspr_free[pp] = s_act.v
            p.wait("scalar", s_act, s_act.v)
            p.emit("scalar", lambda s, q=q, tb=tb, sg_i=sg_i: s.dma_start(
                v_dram[4 * q: 4 * q + 4, :, tb * 128:(tb + 1) * 128]
                .transpose([1, 0, 2]),
                stg[sg_i][:].rearrange("p (h j) -> p h j", h=4)), inc=s_st, k=16)
            stg_free[sg_i] = s_st.v
        if q + 2 < 4:
            dma_wvq(q + 2)

    e_sb = es_a1.enter_context(nc.sbuf_tensor("e_sb", [128, 16], F32))
    l_max = es_a1.enter_context(nc.sbuf_tensor("l_max", [128, 1], F32))
    zacc = es_a1.enter_context(nc.sbuf_tensor("zacc", [128, 1], F32))
    max8t = es_a1.enter_context(nc.sbuf_tensor("max8t", [128, 8], F32))
    s6 = es_a1.enter_context(nc.sbuf_tensor("s6", [128, 1], F32))
    den = es_a1.enter_context(nc.sbuf_tensor("den", [128, 1], F32))
    rden = es_a1.enter_context(nc.sbuf_tensor("rden", [128, 1], F32))
    tmp16a = es_a1.enter_context(nc.sbuf_tensor("tmp16a", [128, 16], F32))
    tmp16b = es_a1.enter_context(nc.sbuf_tensor("tmp16b", [128, 16], F32))
    for tb in range(NTB):
        p.wait("tensor", s_act, s_act.v)
        for ic in range(NI):
            p.emit("tensor", lambda t, tb=tb, ic=ic: t.matmul(
                ps_gt, xn_fm[:, ic * SEQ + tb * 128: ic * SEQ + (tb + 1) * 128],
                gw_sb[:, ic * 16:(ic + 1) * 16],
                start=(ic == 0), stop=(ic == NI - 1)),
                inc=(s_pe if ic == NI - 1 else None))
        p.wait("vector", s_pe, s_pe.v)
        p.emit("vector", lambda v: v.reduce_max(l_max[:], ps_gt, axis=AX.X,
                                                negate=True), inc=s_dve)
        p.wait("scalar", s_dve, s_dve.v)
        p.emit("scalar", lambda s: s.activation(e_sb[:], ps_gt, AF.Exp,
                                                bias=l_max[:], accum_out=zacc[:]),
               inc=s_act)
        p.wait("vector", s_act, s_act.v)
        p.emit("vector", lambda v: v.max(max8t[:], e_sb[:]), inc=s_dve)
        p.wait("vector", s_dve, s_dve.v)
        p.emit("vector", lambda v: v.reduce_sum(s6[:], max8t[:, 0:6], axis=AX.X),
               inc=s_dve)
        p.wait("vector", s_dve, s_dve.v)
        p.emit("vector", lambda v: v.tensor_scalar(den[:], zacc[:], 1e-20, s6[:],
                                                   OP.mult, OP.add), inc=s_dve)
        p.wait("vector", s_dve, s_dve.v)
        p.emit("vector", lambda v: v.reciprocal(rden[:], den[:]), inc=s_dve)
        p.wait("vector", s_dve, s_dve.v)
        p.emit("vector", lambda v, tb=tb: v.tensor_scalar(
            mask_all[:, tb * 16:(tb + 1) * 16], e_sb[:], max8t[:, 5:6],
            None, OP.is_ge), inc=s_dve)
        p.emit("vector", lambda v: v.tensor_scalar(e_sb[:], e_sb[:], rden[:],
                                                   None, OP.mult), inc=s_dve)
        p.wait("vector", s_dve, s_dve.v)
        p.emit("vector", lambda v, tb=tb: v.tensor_tensor(
            comb[:, tb * 16:(tb + 1) * 16], e_sb[:],
            mask_all[:, tb * 16:(tb + 1) * 16], OP.mult), inc=s_dve)
    if debug:
        p.wait("gpsimd", s_dve, s_dve.v)
        p.emit("gpsimd", lambda g: g.dma_start(dbg_comb, comb[:]), inc=s_st, k=16)

    mask_done = s_dve.v
    for tb in range(NTB):
        p.wait("tensor", s_dve, mask_done)
        p.wait("tensor", s_dve, s_dve.v)
        for tc in range(tb + 1):
            p.emit("tensor", lambda t, tb=tb, tc=tc: t.matmul(
                ps_gt, tri_bf[:] if tc == tb else eyeones_bf[:, 128:256],
                mask_all[:, tc * 16:(tc + 1) * 16],
                start=(tc == 0), stop=(tc == tb)),
                inc=(s_pe if tc == tb else None))
        p.wait("vector", s_pe, s_pe.v)
        p.emit("vector", lambda v: v.tensor_scalar(tmp16a[:], ps_gt, 1.0,
                                                   None, OP.add), inc=s_dve)
        p.wait("vector", s_dve, s_dve.v)
        p.emit("vector", lambda v, tb=tb: v.tensor_tensor(
            tmp16b[:], tmp16a[:], mask_all[:, tb * 16:(tb + 1) * 16], OP.mult),
            inc=s_dve)
        p.wait("vector", s_dve, s_dve.v)
        p.emit("vector", lambda v, tb=tb: v.tensor_scalar(
            mposm1[:, tb * 20: tb * 20 + 16], tmp16b[:], -2.0, None, OP.add),
            inc=s_dve)
    mpos_done = s_dve.v
    p.wait("tensor", s_dve, mpos_done)
    for tb in range(NTB):
        par = tb % 2
        p.wait("tensor", s_dve, s_dve.v if tb >= 2 else 0)
        p.emit("tensor", lambda t, tb=tb, par=par: t.transpose(
            ps_pr[par][0:20, 0:128], mposm1[:, tb * 20:(tb + 1) * 20],
            eye_f32[:]), inc=s_pe)
        p.wait("vector", s_pe, s_pe.v)
        p.emit("vector", lambda v, tb=tb, par=par: v.tensor_copy(
            mT_sb[:, tb * 128:(tb + 1) * 128], ps_pr[par][0:20, 0:128]), inc=s_dve)

    a_pe_final = s_pe.v
    a_act_final = s_act.v
    a_dve_final = s_dve.v
    a_st_final = s_st.v
    es_a1.close()
    es_a.close()

    CAP = 512
    NSC = CAP // 128
    EIDX = list(range(16)) + [16, 16, 17, 17]
    MCOL = list(range(16)) + [16, 17, 18, 19]
    NPE = 20
    es_c = ExitStack()
    sbc = lambda n, s, d: es_c.enter_context(nc.sbuf_tensor(n, s, d))
    acc = sbc("acc", [128, NTB * DM], F32)
    es_moe = ExitStack()
    sbm = lambda n, s, d: es_moe.enter_context(nc.sbuf_tensor(n, s, d))
    xn_tm_sb = sbm("xn_tm_sb", [128, NTB * DM], BF16)
    w13r = [sbm(f"w13r{i}", [128, 4096], BF16) for i in range(2)]
    w2r = [sbm(f"w2r{i}", [128, NJ * 512], BF16) for i in range(2)]
    sel_sb = sbm("sel_sb", [128, NTB * CAP], BF16)
    selT_sb = sbm("selT_sb", [128, NSC * TOK], BF16)
    xng_sb = sbm("xng_sb", [128, NI * CAP], BF16)
    hg_sb = sbm("hg_sb", [128, NJ * CAP], BF16)
    yg_b = [sbm(f"yg_b{i}", [128, NSC * 512], BF16) for i in range(2)]
    sg_b = [sbm(f"sg_b{i}", [128, CAP], F32) for i in range(2)]
    mrow_sb = sbm("mrow_sb", [1, TOK], F32)
    ps_g = es_moe.enter_context(nc.psum_tensor("ps_g", [128, 1024], F32))
    ps_u = es_moe.enter_context(nc.psum_tensor("ps_u", [128, 1024], F32))
    ps_y = [es_moe.enter_context(nc.psum_tensor(f"ps_y{i}", [128, 512], F32))
            for i in range(2)]
    ps_sc = [es_moe.enter_context(nc.psum_tensor(f"ps_sc{i}", [128, 512], F32))
             for i in range(2)]
    sc_slots = [ps_sc[0][:], ps_sc[1][:], None, None]

    sc_slots[2] = ps_g[:, 0:512]
    sc_slots[3] = ps_g[:, 512:1024]
    fence("gpsimd", s_pe, s_act, s_dve, s_st)
    fence("vector", s_st)
    fence("sync", s_pe, s_act, s_dve)
    p.emit("sync", lambda s: s.dma_start(
        xn_tm_sb[:].rearrange("p (a c) -> p a c", c=DM),
        xn_tm_dram.rearrange("(a p) c -> p a c", p=128)), inc=s_ld, k=16)
    xn_tm_ready = s_ld.v

    w13_free = [0, 0]
    w2_free = [0, 0]
    w13_ready = {}
    w2_ready = {}
    sg_free = [0, 0]
    psg_free = 0
    ga_free = [0, 0]
    psu_free = [0, 0]
    posb_ps_free = 0
    psy_free = [0, 0]
    sc_free = [0, 0, 0, 0]
    acc_ready = {}
    sel_free = 0
    mrow_free = 0
    selT_free = 0
    xng_free = 0
    hg_free = 0
    yg_free = [0, 0]
    gj = 0
    gq = 0

    def dma_w13(l, j):
        slot = (l * NJ + j) % 2
        p.wait("gpsimd", s_pe, w13_free[slot])
        p.emit("gpsimd", lambda g, l=l, j=j, slot=slot: g.dma_start(
            w13r[slot][:], w13_img[EIDX[l], j]), inc=s_wd, k=16)
        w13_ready[(l, j)] = s_wd.v

    def dma_w2(l, q):
        slot = (l * 4 + q) % 2
        p.wait("gpsimd", s_pe, w2_free[slot])
        p.emit("gpsimd", lambda g, l=l, q=q, slot=slot: g.dma_start(
            w2r[slot][:], w2_img[EIDX[l], q]), inc=s_wd, k=16)
        w2_ready[(l, q)] = s_wd.v

    for k in range(2):
        dma_w13(0, k)
    dma_w2(0, 0)
    dma_w2(0, 1)

    def emit_sel(l):
        mc = MCOL[l]
        p.wait("vector", s_pe, sel_free)
        for tb in range(NTB):
            p.emit("vector", lambda v, tb=tb, mc=mc: v.tensor_scalar(
                sel_sb[:, tb * CAP:(tb + 1) * CAP], iotaS_sb[:, 0:CAP],
                mposm1[:, tb * 20 + mc: tb * 20 + mc + 1], None, OP.is_equal),
                inc=s_dve)
        return s_dve.v

    sel_done = emit_sel(0)
    for l in range(NPE):
        mc = MCOL[l]
        p.wait("gpsimd", s_pe, mrow_free)
        p.emit("gpsimd", lambda g, mc=mc: g.dma_start(
            mrow_sb[:], mT_sb[mc:mc + 1, :]), inc=s_wd, k=16)
        mrow_ready = s_wd.v
        p.wait("tensor", s_dve, max(psu_free))
        p.wait("tensor", s_dve, posb_ps_free)
        p.wait("tensor", s_wd, mrow_ready)
        for half in range(2):
            p.emit("tensor", lambda t, half=half: t.matmul(
                ps_u[:, half * 512:(half + 1) * 512],
                ones_f32r[:],
                mrow_sb[:, half * 512:(half + 1) * 512],
                start=True, stop=True), inc=(s_pe if half == 1 else None))
        mrow_free = s_pe.v
        p.wait("vector", s_pe, s_pe.v)
        p.wait("vector", s_pe, selT_free)
        for sc in range(NSC):
            p.emit("vector", lambda v, sc=sc: v.tensor_scalar(
                selT_sb[:, sc * TOK:(sc + 1) * TOK], ps_u[:],
                iotaCol_sb[:, sc:sc + 1], None, OP.is_equal), inc=s_dve)
        selT_done = s_dve.v
        posb_ps_free = s_dve.v
        p.wait("tensor", s_dve, sel_done)
        if l == 0:
            p.wait("tensor", s_ld, xn_tm_ready)
        p.wait("tensor", s_pe, xng_free)
        for dc in range(NI):
            par = dc % 2
            p.wait("tensor", s_act, max(ga_free[par], psg_free))
            p.wait("tensor", s_dve, sc_free[2 + par])
            for tb in range(NTB):
                p.emit("tensor", lambda t, dc=dc, tb=tb, par=par: t.matmul(
                    ps_g[:, par * 512:(par + 1) * 512],
                    xn_tm_sb[:, tb * DM + dc * 128: tb * DM + (dc + 1) * 128],
                    sel_sb[:, tb * CAP:(tb + 1) * CAP],
                    start=(tb == 0), stop=(tb == NTB - 1)),
                    inc=(s_pe if tb == NTB - 1 else None))
            p.wait("scalar", s_pe, s_pe.v)
            p.emit("scalar", lambda s, dc=dc, par=par: s.activation(
                xng_sb[:, dc * CAP:(dc + 1) * CAP], ps_g[:, par * 512:(par + 1) * 512],
                AF.Identity), inc=s_act)
            ga_free[par] = s_act.v
        sel_free = s_pe.v
        ga_done = s_act.v
        for j in range(NJ):
            slot = (l * NJ + j) % 2
            par = gj % 2
            p.wait("tensor", s_wd, w13_ready[(l, j)])
            p.wait("tensor", s_act, max(psg_free, ga_done))
            for ic in range(NI):
                p.emit("tensor", lambda tt, slot=slot, ic=ic: tt.matmul(
                    ps_g[:, 0:512],
                    w13r[slot][:, ic * 128:(ic + 1) * 128],
                    xng_sb[:, ic * CAP:(ic + 1) * CAP],
                    start=(ic == 0), stop=(ic == NI - 1)),
                    inc=(s_pe if ic == NI - 1 else None))
            pe_g_done = s_pe.v
            uh = j % 2
            p.wait("tensor", s_dve, psu_free[uh])
            p.wait("tensor", s_dve, posb_ps_free)
            for ic in range(NI):
                p.emit("tensor", lambda tt, slot=slot, ic=ic, uh=uh: tt.matmul(
                    ps_u[:, uh * 512:(uh + 1) * 512],
                    w13r[slot][:, 2048 + ic * 128: 2048 + (ic + 1) * 128],
                    xng_sb[:, ic * CAP:(ic + 1) * CAP],
                    start=(ic == 0), stop=(ic == NI - 1)),
                    inc=(s_pe if ic == NI - 1 else None))
            pe_u_done = s_pe.v
            w13_free[slot] = s_pe.v
            p.wait("scalar", s_pe, pe_g_done)
            p.wait("scalar", s_dve, sg_free[par])
            p.emit("scalar", lambda s, par=par: s.activation(
                sg_b[par][:], ps_g[:, 0:512], AF.Silu), inc=s_act)
            psg_free = s_act.v
            p.wait("vector", s_act, s_act.v)
            p.wait("vector", s_pe, pe_u_done)
            if j == 0:
                p.wait("vector", s_pe, hg_free)
            p.emit("vector", lambda v, j=j, par=par, uh=uh: v.tensor_tensor(
                hg_sb[:, j * CAP:(j + 1) * CAP], sg_b[par][:],
                ps_u[:, uh * 512:(uh + 1) * 512],
                OP.mult), inc=s_dve)
            psu_free[uh] = s_dve.v
            sg_free[par] = s_dve.v
            gj += 1
            nxt = l * NJ + j + 2
            if nxt < NPE * NJ:
                dma_w13(nxt // NJ, nxt % NJ)
        h_done = s_dve.v
        xng_free = s_pe.v
        if l + 1 < NPE:
            sel_done = emit_sel(l + 1)
        p.wait("tensor", s_dve, selT_done)
        yg_q_done = {}

        def emit_scatter(l, q):
            qpar = (l * 4 + q) % 2
            p.wait("tensor", s_act, yg_q_done[q])
            for tb in range(NTB):
                par = tb % 4
                slot = sc_slots[par]
                p.wait("tensor", s_dve, sc_free[par])
                if par >= 2:
                    p.wait("tensor", s_act, psg_free)
                for sc in range(NSC):
                    p.emit("tensor", lambda t, tb=tb, sc=sc, slot=slot, qpar=qpar: t.matmul(
                        slot,
                        selT_sb[:, sc * TOK + tb * 128: sc * TOK + tb * 128 + 128],
                        yg_b[qpar][:, sc * 512:(sc + 1) * 512],
                        start=(sc == 0), stop=(sc == NSC - 1)),
                        inc=(s_pe if sc == NSC - 1 else None))
                p.wait("vector", s_pe, s_pe.v)
                if (tb, q) in acc_ready:
                    p.wait("vector", s_dve, acc_ready[(tb, q)])
                accsl = lambda tb=tb, q=q: acc[:, tb * DM + q * 512: tb * DM + (q + 1) * 512]
                if l == 0:
                    p.emit("vector", lambda v, tb=tb, q=q, slot=slot, l=l, accsl=accsl:
                           v.tensor_scalar(accsl(), slot,
                                           comb[:, tb * 16 + l: tb * 16 + l + 1],
                                           None, OP.mult), inc=s_dve)
                elif l < 16:
                    p.emit("vector", lambda v, tb=tb, q=q, slot=slot, l=l, accsl=accsl:
                           v.scalar_tensor_tensor(accsl(), slot,
                                                  comb[:, tb * 16 + l: tb * 16 + l + 1],
                                                  accsl(), OP.mult, OP.add), inc=s_dve)
                else:
                    p.emit("vector", lambda v, tb=tb, q=q, slot=slot, accsl=accsl:
                           v.tensor_tensor(accsl(), slot, accsl(), OP.add),
                           inc=s_dve)
                sc_free[par] = s_dve.v
                acc_ready[(tb, q)] = s_dve.v
            return s_pe.v

        for q in range(4):
            slot = (l * 4 + q) % 2
            qpar = (l * 4 + q) % 2
            p.wait("tensor", s_wd, w2_ready[(l, q)])
            p.wait("tensor", s_pe, yg_free[qpar])
            for sc in range(NSC):
                par = gq % 2
                p.wait("tensor", s_act, psy_free[par])
                if sc == 0 and q == 0:
                    p.wait("tensor", s_dve, h_done)
                for ic in range(NJ):
                    p.emit("tensor", lambda tt, slot=slot, ic=ic, sc=sc, par=par:
                           tt.matmul(ps_y[par][:],
                                     hg_sb[:, ic * CAP + sc * 128: ic * CAP + sc * 128 + 128],
                                     w2r[slot][:, ic * 512:(ic + 1) * 512],
                                     start=(ic == 0), stop=(ic == NJ - 1)),
                           inc=(s_pe if ic == NJ - 1 else None))
                p.wait("scalar", s_pe, s_pe.v)
                p.emit("scalar", lambda s, qpar=qpar, sc=sc, par=par: s.activation(
                    yg_b[qpar][:, sc * 512:(sc + 1) * 512],
                    ps_y[par][:], AF.Identity), inc=s_act)
                psy_free[par] = s_act.v
                gq += 1
            w2_free[slot] = s_pe.v
            yg_q_done[q] = s_act.v
            nxtq = l * 4 + q + 2
            if nxtq < NPE * 4:
                dma_w2(nxtq // 4, nxtq % 4)
            if q >= 1:
                yg_free[(l * 4 + q - 1) % 2] = emit_scatter(l, q - 1)
        yg_free[(l * 4 + 3) % 2] = emit_scatter(l, 3)
        hg_free = s_pe.v
        selT_free = s_pe.v

    c_pe_final = s_pe.v
    c_dve_final = s_dve.v
    c_act_final = s_act.v
    if debug:
        p.wait("gpsimd", s_dve, s_dve.v)
        p.emit("gpsimd", lambda g: g.dma_start(dbg_accC, acc[:]), inc=s_st, k=16)
    es_moe.close()

    es_b = ExitStack()
    sbb = lambda n, s, d: es_b.enter_context(nc.sbuf_tensor(n, s, d))
    o_all = sbb("o_all", [128, NH * TOK], BF16)
    wo_sb = [sbb(f"wo_sb{i}", [128, NH * 512], BF16) for i in range(2)]
    es_b1 = ExitStack()
    sbb1 = lambda n, s, d: es_b1.enter_context(nc.sbuf_tensor(n, s, d))
    k_sb = [sbb1(f"k_sb{i}", [128, SEQ], BF16) for i in range(2)]
    v_sb = [sbb1(f"v_sb{i}", [128, SEQ], BF16) for i in range(2)]
    q_sb = [sbb1(f"q_sb{i}", [128, TOK], BF16) for i in range(2)]
    p_sb = [sbb1(f"p_sb{i}", [128, 1024], BF16) for i in range(2)]
    rb_sb = [sbb1(f"rb_sb{i}", [128, 512], F32) for i in range(2)]
    ps_s = [es_b1.enter_context(nc.psum_tensor(f"ps_s{i}", [128, 1024], F32))
            for i in range(2)]
    ps_oz = [es_b1.enter_context(nc.psum_tensor(f"ps_oz{i}", [128, 1024], F32))
             for i in range(2)]

    fence("sync", s_pe, s_dve, s_act)
    fence("gpsimd", s_pe, s_dve)

    kvq_ready = {}
    kvq_free = [0, 0]

    def dma_head(hh):
        par = hh % 2
        p.wait("sync", s_pe, kvq_free[par])
        p.emit("sync", lambda s, hh=hh, par=par: s.dma_start(k_sb[par][:], k_dram[hh]),
               inc=s_ld, k=16)
        p.emit("sync", lambda s, hh=hh, par=par: s.dma_start(v_sb[par][:], v_dram[hh]),
               inc=s_ld, k=16)
        p.emit("sync", lambda s, hh=hh, par=par: s.dma_start(q_sb[par][:], q_dram[hh]),
               inc=s_ld, k=16)
        kvq_ready[hh] = s_ld.v

    dma_head(0)
    dma_head(1)
    wo_ready = {}
    wo_free = [0, 0]

    def dma_wo(q):
        slot = q % 2
        p.wait("gpsimd", s_pe, wo_free[slot])
        p.emit("gpsimd", lambda g, q=q, slot=slot: g.dma_start(
            wo_sb[slot][:], wo_img[q]), inc=s_wd, k=16)
        wo_ready[q] = s_wd.v

    dma_wo(0)
    dma_wo(1)

    pss_free = [0, 0]
    pso_free = 0
    psz_free = 0
    p_free = [0, 0]
    NKC = SEQ // 128
    NPC = NKC // 2

    def emit_pv_pair(par, pc, opar):
        pr = pc % 2
        for half in range(2):
            kc = 2 * pc + half
            p.emit("tensor", lambda t, par=par, kc=kc, pr=pr, half=half, opar=opar:
                   t.matmul(ps_oz[opar][:, 0:512],
                            v_sb[par][:, kc * 128:(kc + 1) * 128],
                            p_sb[pr][:, half * 512:(half + 1) * 512],
                            start=(kc == 0), stop=(kc == NKC - 1)))
            p.emit("tensor", lambda t, kc=kc, pr=pr, half=half, opar=opar: t.matmul(
                ps_oz[opar][:, 512:1024], eyeones_bf[:, 128:256],
                p_sb[pr][:, half * 512:(half + 1) * 512],
                start=(kc == 0), stop=(kc == NKC - 1)),
                inc=(s_pe if half == 1 else None))
        p_free[pr] = s_pe.v

    pso_free2 = [0, 0]
    for hh in range(NH):
        par = hh % 2
        p.wait("tensor", s_ld, kvq_ready[hh])
        for qt in range(2):
            opar = (hh * 2 + qt) % 2
            p.wait("tensor", s_dve, pso_free2[opar])
            exp_done = {}
            for pc in range(NPC):
                sp = pc % 2
                pr = pc % 2
                p.wait("tensor", s_act, pss_free[sp])
                for half in range(2):
                    kc = 2 * pc + half
                    p.emit("tensor", lambda t, par=par, qt=qt, kc=kc, sp=sp, half=half:
                           t.matmul(ps_s[sp][:, half * 512:(half + 1) * 512],
                                    k_sb[par][:, kc * 128:(kc + 1) * 128],
                                    q_sb[par][:, qt * 512:(qt + 1) * 512],
                                    start=True, stop=True),
                           inc=(s_pe if half == 1 else None))
                p.wait("scalar", s_pe, s_pe.v)
                p.wait("scalar", s_pe, p_free[pr])
                p.emit("scalar", lambda s, sp=sp, pr=pr: s.activation(
                    p_sb[pr][:], ps_s[sp][:], AF.Exp, bias=c_neg12[:]), inc=s_act)
                pss_free[sp] = s_act.v
                exp_done[pc] = s_act.v
                if pc >= 1:
                    p.wait("tensor", s_act, exp_done[pc - 1])
                    emit_pv_pair(par, pc - 1, opar)
            p.wait("tensor", s_act, exp_done[NPC - 1])
            emit_pv_pair(par, NPC - 1, opar)
            pv_done = s_pe.v
            p.wait("vector", s_pe, pv_done)
            p.emit("vector", lambda v, opar=opar: v.reciprocal(
                rb_sb[opar][:], ps_oz[opar][:, 512:1024]), inc=s_dve)
            p.wait("vector", s_dve, s_dve.v)
            p.emit("vector", lambda v, hh=hh, qt=qt, opar=opar: v.tensor_tensor(
                o_all[:, hh * TOK + qt * 512: hh * TOK + (qt + 1) * 512],
                ps_oz[opar][:, 0:512], rb_sb[opar][:], OP.mult), inc=s_dve)
            pso_free2[opar] = s_dve.v
        kvq_free[par] = s_pe.v
        if hh + 2 < NH:
            dma_head(hh + 2)

    b1_pe = s_pe.v
    b1_dve = s_dve.v
    if debug:
        p.wait("gpsimd", s_dve, s_dve.v)
        p.emit("gpsimd", lambda g: g.dma_start(dbg_oall, o_all[:]), inc=s_st, k=16)
    es_b1.close()

    es_b2 = ExitStack()
    ps_w = [es_b2.enter_context(nc.psum_tensor(f"ps_w{i}", [128, 512], F32))
            for i in range(2)]
    psw_free = [0, 0]
    p.wait("tensor", s_dve, b1_dve)
    p.wait("tensor", s_act, s_act.v)
    for q in range(4):
        slot = q % 2
        p.wait("tensor", s_wd, wo_ready[q])
        for tb in range(NTB):
            par = (q * NTB + tb) % 2
            p.wait("tensor", s_dve, psw_free[par])
            for hh in range(NH):
                p.emit("tensor", lambda t, q=q, slot=slot, tb=tb, hh=hh, par=par:
                       t.matmul(ps_w[par][:, 0:512],
                                o_all[:, hh * TOK + tb * 128: hh * TOK + tb * 128 + 128],
                                wo_sb[slot][:, hh * 512:(hh + 1) * 512],
                                start=(hh == 0), stop=(hh == NH - 1)),
                       inc=(s_pe if hh == NH - 1 else None))
            p.wait("vector", s_pe, s_pe.v)
            p.wait("vector", s_dve, acc_ready[(tb, q)])
            p.emit("vector", lambda v, q=q, tb=tb, par=par: v.scalar_tensor_tensor(
                acc[:, tb * DM + q * 512: tb * DM + (q + 1) * 512],
                ps_w[par][:, 0:512], 1.0,
                acc[:, tb * DM + q * 512: tb * DM + (q + 1) * 512],
                OP.mult, OP.add), inc=s_dve)
            psw_free[par] = s_dve.v
            acc_ready[(tb, q)] = s_dve.v
        wo_free[slot] = s_pe.v
        if q + 2 < 4:
            dma_wo(q + 2)
    es_b2.close()
    es_b.close()

    es_d = ExitStack()
    xo_t = [es_d.enter_context(nc.sbuf_tensor(f"xo_t{i}", [128, DM], F32))
            for i in range(2)]
    o_t = [es_d.enter_context(nc.sbuf_tensor(f"o_t{i}", [128, DM], F32))
           for i in range(2)]
    fence("sync", s_pe, s_dve)
    ot_free = [0, 0]
    xo_free = [0, 0]
    for tb in range(NTB):
        par = tb % 2
        p.wait("sync", s_dve, xo_free[par])
        p.emit("sync", lambda s, tb=tb, par=par: s.dma_start(
            xo_t[par][:], xb[tb * 128:(tb + 1) * 128, :]), inc=s_ld, k=16)
        p.wait("vector", s_ld, s_ld.v)
        p.wait("vector", s_dve, max(acc_ready[(tb, q)] for q in range(4)))
        p.wait("vector", s_st, ot_free[par])
        p.emit("vector", lambda v, tb=tb, par=par: v.tensor_tensor(
            o_t[par][:], acc[:, tb * DM:(tb + 1) * DM], xo_t[par][:], OP.add),
            inc=s_dve)
        xo_free[par] = s_dve.v
        p.wait("sync", s_dve, s_dve.v)
        p.emit("sync", lambda s, tb=tb, par=par: s.dma_start(
            out[tb * 128:(tb + 1) * 128, :], o_t[par][:]), inc=s_st, k=16)
        ot_free[par] = s_st.v
    p.wait("sync", s_st, s_st.v)
    p.wait("gpsimd", s_st, s_st.v)

    es_d.close()
    es_c.close()

    with nc.Block() as blk:
        p.into(blk)
    es_g.close()
    n_inst = sum(len(v) for v in p.ops.values())
    return nc, n_inst


def _images(L0_attn_wq, L0_attn_wk, L0_attn_wv, L0_attn_wo,
            L0_ffn_gate_w, L0_ffn_w13, L0_ffn_w2, L0_ffn_sw13, L0_ffn_sw2):
    f32 = np.float32
    wq = np.asarray(L0_attn_wq, f32)
    wk = np.asarray(L0_attn_wk, f32)
    wv = np.asarray(L0_attn_wv, f32)
    wo = np.asarray(L0_attn_wo, f32)
    gw = np.asarray(L0_ffn_gate_w, f32)
    w13 = np.asarray(L0_ffn_w13, f32)
    w2 = np.asarray(L0_ffn_w2, f32)
    sw13 = np.asarray(L0_ffn_sw13, f32)
    sw2 = np.asarray(L0_ffn_sw2, f32)

    def colblocks(w):
        return np.ascontiguousarray(
            w.reshape(NI, 128, NI, 128).transpose(0, 3, 2, 1).reshape(NI, 128, DM))
    wq_img = colblocks(wq)
    wk_img = colblocks(wk)
    wvq_img = np.ascontiguousarray(
        wv.reshape(4, 512, NI, 128).transpose(0, 3, 2, 1).reshape(4, 128, NI * 512))
    wo_img = np.ascontiguousarray(
        wo.reshape(4, 512, NH, 128).transpose(0, 3, 2, 1).reshape(4, 128, NH * 512))
    gw_img = np.ascontiguousarray(
        gw.T.reshape(NI, 128, 16).transpose(1, 0, 2).reshape(128, NI * 16))
    w13v = w13.reshape(16, 2, NJ, 128, NI, 128)
    w13_r = w13v.transpose(0, 2, 5, 1, 4, 3).reshape(16, NJ, 128, 4096)
    sw13v = sw13.reshape(2, 2, NJ, 128, NI, 128)
    sw13_r = sw13v.transpose(1, 2, 5, 0, 4, 3).reshape(2, NJ, 128, 4096)
    w13_img = np.ascontiguousarray(np.concatenate([w13_r, sw13_r], axis=0))
    w2v = w2.reshape(16, 4, 512, NJ, 128)
    w2_r = w2v.transpose(0, 1, 4, 3, 2).reshape(16, 4, 128, NJ * 512)
    sw2v = sw2.reshape(4, 512, 2, NJ, 128)
    sw2_r = sw2v.transpose(2, 0, 4, 3, 1).reshape(2, 4, 128, NJ * 512)
    w2_img = np.ascontiguousarray(np.concatenate([w2_r, sw2_r], axis=0))

    eyeones = np.concatenate(
        [np.eye(128, dtype=f32), np.ones((128, 128), f32)], axis=1)
    iotaS = np.broadcast_to(np.arange(512, dtype=f32)[None, :], (128, 512))
    pcol = np.arange(128, dtype=f32)[:, None]
    iotaCol = pcol + 512.0 * 0 + np.arange(4, dtype=f32)[None, :] * 128.0
    tri = (np.arange(128)[:, None] <= np.arange(128)[None, :]).astype(f32)
    mconst = np.full((128, 8, 4), -2.0, f32)
    for tb in range(8):
        for q in range(4):
            first = (q % 2 == 0)
            if first and tb < 4:
                mconst[:, tb, q] = tb * 128 + pcol[:, 0]
            if (not first) and tb >= 4:
                mconst[:, tb, q] = (tb - 4) * 128 + pcol[:, 0]
    consts2 = np.ascontiguousarray(np.concatenate(
        [iotaS, iotaCol, tri, mconst.reshape(128, 32)], axis=1))

    return {"wq_img": wq_img, "wk_img": wk_img, "wvq_img": wvq_img,
            "wo_img": wo_img, "gw_img": gw_img, "w13_img": w13_img,
            "w2_img": w2_img, "eyeones": eyeones, "consts2": consts2}


def kernel(L0_attn_x, L0_attn_wq, L0_attn_wk, L0_attn_wv, L0_attn_wo,
           L0_ffn_gate_w, L0_ffn_w13, L0_ffn_w2, L0_ffn_sw13, L0_ffn_sw2):
    x = np.asarray(L0_attn_x, np.float32)
    imgs = _images(L0_attn_wq, L0_attn_wk, L0_attn_wv, L0_attn_wo,
                   L0_ffn_gate_w, L0_ffn_w13, L0_ffn_w2, L0_ffn_sw13, L0_ffn_sw2)
    nc, _ = build_program()
    in_maps = []
    for c in range(NCORES):
        b, qs = c // 4, c % 4
        xbat = x[b]
        own = xbat[qs * TOK:(qs + 1) * TOK]
        rest = np.concatenate([xbat[:qs * TOK], xbat[(qs + 1) * TOK:]], axis=0)
        xb_c = np.ascontiguousarray(np.concatenate([own, rest], axis=0))
        in_maps.append({"xb": xb_c, **imgs})
    res = run_bass_kernel_spmd(nc, in_maps, core_ids=list(range(NCORES)))
    outp = np.empty((BATCH, SEQ, DM), np.float32)
    for c in range(NCORES):
        b, qs = c // 4, c % 4
        outp[b, qs * TOK:(qs + 1) * TOK] = res.results[c]["out"]
    return outp


# version 18
# speedup vs baseline: 1.0201x; 1.0198x over previous
import numpy as np
import ml_dtypes
from contextlib import ExitStack
import concourse.bass as bass
import concourse.mybir as mybir
from concourse.bass_utils import run_bass_kernel_spmd
from concourse.alu_op_type import AluOpType as OP

F32 = mybir.dt.float32
BF16 = mybir.dt.bfloat16
AF = mybir.ActivationFunctionType
AX = mybir.AxisListType
BFNP = ml_dtypes.bfloat16

NCORES = 8
BATCH, SEQ, DM = 2, 4096, 2048
TOK = 1024
NI = DM // 128
NH = 16
HD = 128
SCALE = 1.0 / np.sqrt(HD).astype(np.float32)
I_HID = 1408
NJ = I_HID // 128
NEX = 18
NTB = TOK // 128
TBF = SEQ // 128
EXP_SHIFT = -12.0


class Sem:
    __slots__ = ("h", "v")

    def __init__(self, handle):
        self.h = handle
        self.v = 0


class Plan:
    ENGINES = ("tensor", "vector", "scalar", "gpsimd", "sync")

    def __init__(self):
        self.ops = {e: [] for e in self.ENGINES}

    def emit(self, eng, fn, inc=None, k=1):
        if inc is not None:
            inc.v += k
            self.ops[eng].append(lambda e, fn=fn, s=inc, k=k: fn(e).then_inc(s.h, k))
        else:
            self.ops[eng].append(lambda e, fn=fn: fn(e))

    def wait(self, eng, sem, v=None):
        v = sem.v if v is None else v
        if v > 0:
            self.ops[eng].append(lambda e, s=sem, v=v: e.wait_ge(s.h, v))

    def into(self, block):
        for eng in self.ENGINES:
            ops = self.ops[eng]
            if ops:
                getattr(block, eng)(lambda e, ops=ops: [f(e) for f in ops])


def build_program(debug=False):
    nc = bass.Bass("TRN2", target_bir_lowering=False)
    scratch_kind = "ExternalOutput" if debug else "Internal"

    xb = nc.dram_tensor("xb", [SEQ, DM], F32, kind="ExternalInput").ap()
    wq_img = nc.dram_tensor("wq_img", [NI, 128, DM], F32, kind="ExternalInput").ap()
    wk_img = nc.dram_tensor("wk_img", [NI, 128, DM], F32, kind="ExternalInput").ap()
    wvq_img = nc.dram_tensor("wvq_img", [4, 128, NI * 512], F32, kind="ExternalInput").ap()
    wo_img = nc.dram_tensor("wo_img", [4, 128, NH * 512], F32, kind="ExternalInput").ap()
    gw_img = nc.dram_tensor("gw_img", [128, NI * 16], F32, kind="ExternalInput").ap()
    w13_img = nc.dram_tensor("w13_img", [NEX, NJ, 128, 4096], F32, kind="ExternalInput").ap()
    w2_img = nc.dram_tensor("w2_img", [NEX, 4, 128, NJ * 512], F32, kind="ExternalInput").ap()
    eyeones = nc.dram_tensor("eyeones", [128, 256], F32, kind="ExternalInput").ap()
    consts2 = nc.dram_tensor("consts2", [128, 676], F32, kind="ExternalInput").ap()

    out = nc.dram_tensor("out", [TOK, DM], F32, kind="ExternalOutput").ap()

    k_dram = nc.dram_tensor("k_dram", [NH, 128, SEQ], BF16, kind=scratch_kind).ap()
    v_dram = nc.dram_tensor("v_dram", [NH, 128, SEQ], BF16, kind=scratch_kind).ap()
    q_dram = nc.dram_tensor("q_dram", [NH, 128, TOK], BF16, kind=scratch_kind).ap()
    xn_tm_dram = nc.dram_tensor("xn_tm_dram", [TOK, DM], BF16, kind=scratch_kind).ap()
    if debug:
        dbg_xn = nc.dram_tensor("dbg_xn", [128, NI * TOK], F32, kind="ExternalOutput").ap()
        dbg_comb = nc.dram_tensor("dbg_comb", [128, NTB * 16], F32, kind="ExternalOutput").ap()
        dbg_accC = nc.dram_tensor("dbg_accC", [128, NTB * DM], F32, kind="ExternalOutput").ap()
        dbg_oall = nc.dram_tensor("dbg_oall", [128, NH * TOK], F32, kind="ExternalOutput").ap()

    es_g = ExitStack()
    sb = lambda n, s, d: es_g.enter_context(nc.sbuf_tensor(n, s, d))
    eyeones_bf = sb("eyeones_bf", [128, 256], BF16)
    ones_f32r = sb("ones_f32r", [1, 128], F32)
    eye_f32 = sb("eye_f32", [128, 128], F32)
    c_eps = sb("c_eps", [128, 1], F32)
    c_neg12 = sb("c_neg12", [128, 1], F32)
    comb = sb("comb", [128, NTB * 16], F32)
    mask_all = sb("mask_all", [128, NTB * 16], BF16)
    mposm1 = sb("mposm1", [128, NTB * 20], F32)
    mT_sb = sb("mT_sb", [20, NTB * 128], F32)
    iotaS_sb = sb("iotaS_sb", [128, 512], F32)
    iotaCol_sb = sb("iotaCol_sb", [128, 4], F32)
    tri_bf = sb("tri_bf", [128, 128], BF16)

    sems = {k: Sem(es_g.enter_context(nc.semaphore(k))) for k in
            ("wd", "pe", "act", "dve", "ld", "st", "io")}
    s_wd, s_pe, s_act, s_dve = sems["wd"], sems["pe"], sems["act"], sems["dve"]
    s_ld, s_st, s_io = sems["ld"], sems["st"], sems["io"]

    p = Plan()

    def fence(eng, *ss):
        for s in ss:
            p.wait(eng, s)

    p.emit("gpsimd", lambda g: g.dma_start(eyeones_bf[:], eyeones), inc=s_wd, k=16)
    p.emit("gpsimd", lambda g: g.dma_start(ones_f32r[:], eyeones[0:1, 128:256]), inc=s_wd, k=16)
    p.emit("gpsimd", lambda g: g.dma_start(iotaS_sb[:], consts2[:, 0:512]), inc=s_wd, k=16)
    p.emit("gpsimd", lambda g: g.dma_start(iotaCol_sb[:], consts2[:, 512:516]), inc=s_wd, k=16)
    p.emit("gpsimd", lambda g: g.dma_start(tri_bf[:], consts2[:, 516:644]), inc=s_wd, k=16)
    p.emit("gpsimd", lambda g: g.dma_start(
        mposm1[:].rearrange("p (a c) -> p a c", c=20)[:, :, 16:20],
        consts2[:, 644:676].rearrange("p (a c) -> p a c", c=4)), inc=s_wd, k=16)
    p.emit("gpsimd", lambda g: g.dma_start(eye_f32[:], eyeones[:, 0:128]), inc=s_wd, k=16)
    p.emit("vector", lambda v: v.memset(c_eps[:], 1e-5), inc=s_dve)
    p.emit("vector", lambda v: v.memset(c_neg12[:], EXP_SHIFT), inc=s_dve)

    es_a = ExitStack()
    sba = lambda n, s, d: es_a.enter_context(nc.sbuf_tensor(n, s, d))
    xn_fm = sba("xn_fm", [128, NI * SEQ], BF16)
    gw_sb = sba("gw_sb", [128, NI * 16], BF16)
    stg = [sba(f"stg{i}", [128, 512], BF16) for i in range(8)]
    ps_pr = [es_a.enter_context(nc.psum_tensor(f"ps_pr{i}", [128, 512], F32)) for i in range(3)]
    ps_gt_full = es_a.enter_context(nc.psum_tensor("ps_gt", [128, 512], F32))
    ps_gt = ps_gt_full[:, 0:16]
    es_a0 = ExitStack()
    sba0 = lambda n, s, d: es_a0.enter_context(nc.sbuf_tensor(n, s, d))
    x_tile = [sba0(f"x_tile{i}", [128, DM], F32) for i in range(2)]
    bnst = [sba0(f"bnst{i}", [128, 24], F32) for i in range(2)]
    mv = [sba0(f"mv{i}", [128, 2], F32) for i in range(2)]
    std = [sba0(f"std{i}", [128, 1], F32) for i in range(2)]
    rstd = [sba0(f"rstd{i}", [128, 1], F32) for i in range(2)]
    negmr = [sba0(f"negmr{i}", [128, 1], F32) for i in range(2)]
    xn_tm = [sba0(f"xn_tm{i}", [128, DM], BF16) for i in range(2)]
    ps_tr = [es_a0.enter_context(nc.psum_tensor(f"ps_tr{i}", [128, 1024], BF16)) for i in range(2)]

    p.emit("gpsimd", lambda g: g.dma_start(gw_sb[:], gw_img), inc=s_wd, k=16)

    tr_free = [0, 0]
    xtile_free = [0, 0]
    xntm_free = [0, 0]
    xntm_st_free = [0, 0]
    ld_val = [0, 0]
    aggr_done = [0, 0]
    small_done = [0, 0]
    act_xn = [0, 0]
    for tb in range(TBF + 1):
        par = tb % 2
        if tb < TBF:
            p.wait("sync", s_act, xtile_free[par])
            p.emit("sync", lambda s, tb=tb, par=par: s.dma_start(
                x_tile[par][:], xb[tb * 128:(tb + 1) * 128, :]), inc=s_ld, k=16)
            ld_val[par] = s_ld.v
            p.wait("vector", s_ld, ld_val[par])
            for kk in range(4):
                p.emit("vector", lambda v, par=par, kk=kk: v.bn_stats(
                    bnst[par][:, kk * 6:(kk + 1) * 6],
                    x_tile[par][:, kk * 512:(kk + 1) * 512]),
                    inc=(s_dve if kk == 3 else None))
            p.wait("vector", s_dve, s_dve.v)
            p.emit("vector", lambda v, par=par: v.bn_aggr(mv[par][:], bnst[par][:]),
                   inc=s_dve)
            aggr_done[par] = s_dve.v
        if tb >= 1:
            pb = (tb - 1) % 2
            tbb = tb - 1
            p.wait("scalar", s_dve, aggr_done[pb])
            p.emit("scalar", lambda s, pb=pb: s.activation(
                std[pb][:], mv[pb][:, 1:2], AF.Sqrt, bias=c_eps[:]), inc=s_act)
            p.wait("vector", s_act, s_act.v)
            p.emit("vector", lambda v, pb=pb: v.reciprocal(rstd[pb][:], std[pb][:]),
                   inc=s_dve)
            p.wait("vector", s_dve, s_dve.v)
            p.emit("vector", lambda v, pb=pb: v.tensor_scalar(
                negmr[pb][:], mv[pb][:, 0:1], rstd[pb][:], -1.0, OP.mult, OP.mult),
                inc=s_dve)
            small_done[pb] = s_dve.v
            p.wait("scalar", s_dve, small_done[pb])
            p.wait("scalar", s_pe, xntm_free[pb])
            p.wait("scalar", s_st, xntm_st_free[pb])
            p.emit("scalar", lambda s, pb=pb: s.activation(
                xn_tm[pb][:], x_tile[pb][:], AF.Identity, bias=negmr[pb][:],
                scale=rstd[pb][:]), inc=s_act)
            xtile_free[pb] = s_act.v
            act_xn[pb] = s_act.v
            if tbb < NTB:
                p.wait("sync", s_act, act_xn[pb])
                p.emit("sync", lambda s, tbb=tbb, pb=pb: s.dma_start(
                    xn_tm_dram[tbb * 128:(tbb + 1) * 128, :], xn_tm[pb][:]),
                    inc=s_st, k=16)
                xntm_st_free[pb] = s_st.v
            p.wait("tensor", s_act, act_xn[pb])
            for ic in range(NI):
                tp = (tbb * NI + ic) % 2
                p.wait("tensor", s_act, tr_free[tp])
                p.emit("tensor", lambda t, pb=pb, ic=ic, tp=tp: t.transpose(
                    ps_tr[tp][:, 0:128], xn_tm[pb][:, ic * 128:(ic + 1) * 128],
                    eyeones_bf[:, 0:128]), inc=s_pe)
                p.wait("scalar", s_pe, s_pe.v)
                p.emit("scalar", lambda s, tbb=tbb, ic=ic, tp=tp: s.activation(
                    xn_fm[:, ic * SEQ + tbb * 128: ic * SEQ + (tbb + 1) * 128],
                    ps_tr[tp][:, 0:128], AF.Identity), inc=s_act)
                tr_free[tp] = s_act.v
            xntm_free[pb] = s_pe.v

    a0_act = s_act.v
    es_a0.close()
    es_aw = ExitStack()
    wblk = [es_aw.enter_context(nc.sbuf_tensor(f"wblk{i}", [128, DM], BF16))
            for i in range(2)]


    wblk_free = [0, 0]
    wblk_ready = {}
    stg_free = [0] * 8
    pspr_free = [0, 0, 0]
    nst = 0

    def dma_wblk(idx, src_ap_fn):
        par = idx % 2
        p.wait("gpsimd", s_pe, wblk_free[par])
        p.emit("gpsimd", lambda g, src_ap_fn=src_ap_fn, par=par: g.dma_start(
            wblk[par][:], src_ap_fn()), inc=s_wd, k=16)
        wblk_ready[idx] = s_wd.v

    def wsrc(i):
        if i < 16:
            return lambda i=i: wk_img[i]
        return lambda i=i: wq_img[i - 16]

    fence("gpsimd", s_act, s_pe)
    dma_wblk(0, wsrc(0))
    dma_wblk(1, wsrc(1))
    for wi in range(32):
        oc = wi % 16
        is_q = wi >= 16
        par = wi % 2
        nt = 2 if is_q else 8
        p.wait("tensor", s_wd, wblk_ready[wi])
        for tt in range(nt):
            pp = (wi * 8 + tt) % 2
            p.wait("tensor", s_act, pspr_free[pp])
            for ic in range(NI):
                p.emit("tensor", lambda t, par=par, ic=ic, tt=tt, pp=pp: t.matmul(
                    ps_pr[pp][:], wblk[par][:, ic * 128:(ic + 1) * 128],
                    xn_fm[:, ic * SEQ + tt * 512: ic * SEQ + (tt + 1) * 512],
                    start=(ic == 0), stop=(ic == NI - 1)),
                    inc=(s_pe if ic == NI - 1 else None))
            sg_i = nst % 8
            nst += 1
            p.wait("scalar", s_pe, s_pe.v)
            p.wait("scalar", s_st, stg_free[sg_i])
            p.emit("scalar", lambda s, pp=pp, sg_i=sg_i, is_q=is_q: s.activation(
                stg[sg_i][:], ps_pr[pp][:], AF.Identity,
                scale=(float(SCALE) if is_q else 1.0)), inc=s_act)
            pspr_free[pp] = s_act.v
            p.wait("sync", s_act, s_act.v)
            if is_q:
                p.emit("sync", lambda s, oc=oc, tt=tt, sg_i=sg_i: s.dma_start(
                    q_dram[oc, :, tt * 512:(tt + 1) * 512], stg[sg_i][:]), inc=s_st, k=16)
            else:
                p.emit("sync", lambda s, oc=oc, tt=tt, sg_i=sg_i: s.dma_start(
                    k_dram[oc, :, tt * 512:(tt + 1) * 512], stg[sg_i][:]), inc=s_st, k=16)
            stg_free[sg_i] = s_st.v
        wblk_free[par] = s_pe.v
        if wi + 2 < 32:
            dma_wblk(wi + 2, wsrc(wi + 2))

    es_aw.close()
    es_a1 = ExitStack()
    wvq_sb = [es_a1.enter_context(nc.sbuf_tensor(f"wvq_sb{i}", [128, NI * 512], BF16))
              for i in range(2)]
    wvq_free = [s_pe.v, s_pe.v]
    wvq_ready = {}

    def dma_wvq(q):
        p.wait("gpsimd", s_pe, wvq_free[q % 2])
        p.emit("gpsimd", lambda g, q=q: g.dma_start(wvq_sb[q % 2][:], wvq_img[q]),
               inc=s_wd, k=16)
        wvq_ready[q] = s_wd.v

    dma_wvq(0)
    dma_wvq(1)
    for q in range(4):
        p.wait("tensor", s_wd, wvq_ready[q])
        for tb in range(TBF):
            pp = (q * TBF + tb) % 3
            p.wait("tensor", s_act, pspr_free[pp])
            for ic in range(NI):
                p.emit("tensor", lambda t, q=q, tb=tb, ic=ic, pp=pp: t.matmul(
                    ps_pr[pp][:],
                    xn_fm[:, ic * SEQ + tb * 128: ic * SEQ + (tb + 1) * 128],
                    wvq_sb[q % 2][:, ic * 512:(ic + 1) * 512],
                    start=(ic == 0), stop=(ic == NI - 1)),
                    inc=(s_pe if ic == NI - 1 else None))
            wvq_free[q % 2] = s_pe.v
            sg_i = nst % 8
            nst += 1
            p.wait("scalar", s_pe, s_pe.v)
            p.wait("scalar", s_st, stg_free[sg_i])
            p.emit("scalar", lambda s, pp=pp, sg_i=sg_i: s.activation(
                stg[sg_i][:], ps_pr[pp][:], AF.Identity), inc=s_act)
            pspr_free[pp] = s_act.v
            p.wait("sync", s_act, s_act.v)
            p.emit("sync", lambda s, q=q, tb=tb, sg_i=sg_i: s.dma_start(
                v_dram[4 * q: 4 * q + 4, :, tb * 128:(tb + 1) * 128]
                .transpose([1, 0, 2]),
                stg[sg_i][:].rearrange("p (h j) -> p h j", h=4)), inc=s_st, k=16)
            stg_free[sg_i] = s_st.v
        if q + 2 < 4:
            dma_wvq(q + 2)

    e_sb = es_a1.enter_context(nc.sbuf_tensor("e_sb", [128, 16], F32))
    l_max = es_a1.enter_context(nc.sbuf_tensor("l_max", [128, 1], F32))
    zacc = es_a1.enter_context(nc.sbuf_tensor("zacc", [128, 1], F32))
    max8t = es_a1.enter_context(nc.sbuf_tensor("max8t", [128, 8], F32))
    s6 = es_a1.enter_context(nc.sbuf_tensor("s6", [128, 1], F32))
    den = es_a1.enter_context(nc.sbuf_tensor("den", [128, 1], F32))
    rden = es_a1.enter_context(nc.sbuf_tensor("rden", [128, 1], F32))
    tmp16a = es_a1.enter_context(nc.sbuf_tensor("tmp16a", [128, 16], F32))
    tmp16b = es_a1.enter_context(nc.sbuf_tensor("tmp16b", [128, 16], F32))
    for tb in range(NTB):
        p.wait("tensor", s_act, s_act.v)
        for ic in range(NI):
            p.emit("tensor", lambda t, tb=tb, ic=ic: t.matmul(
                ps_gt, xn_fm[:, ic * SEQ + tb * 128: ic * SEQ + (tb + 1) * 128],
                gw_sb[:, ic * 16:(ic + 1) * 16],
                start=(ic == 0), stop=(ic == NI - 1)),
                inc=(s_pe if ic == NI - 1 else None))
        p.wait("vector", s_pe, s_pe.v)
        p.emit("vector", lambda v: v.reduce_max(l_max[:], ps_gt, axis=AX.X,
                                                negate=True), inc=s_dve)
        p.wait("scalar", s_dve, s_dve.v)
        p.emit("scalar", lambda s: s.activation(e_sb[:], ps_gt, AF.Exp,
                                                bias=l_max[:], accum_out=zacc[:]),
               inc=s_act)
        p.wait("vector", s_act, s_act.v)
        p.emit("vector", lambda v: v.max(max8t[:], e_sb[:]), inc=s_dve)
        p.wait("vector", s_dve, s_dve.v)
        p.emit("vector", lambda v: v.reduce_sum(s6[:], max8t[:, 0:6], axis=AX.X),
               inc=s_dve)
        p.wait("vector", s_dve, s_dve.v)
        p.emit("vector", lambda v: v.tensor_scalar(den[:], zacc[:], 1e-20, s6[:],
                                                   OP.mult, OP.add), inc=s_dve)
        p.wait("vector", s_dve, s_dve.v)
        p.emit("vector", lambda v: v.reciprocal(rden[:], den[:]), inc=s_dve)
        p.wait("vector", s_dve, s_dve.v)
        p.emit("vector", lambda v, tb=tb: v.tensor_scalar(
            mask_all[:, tb * 16:(tb + 1) * 16], e_sb[:], max8t[:, 5:6],
            None, OP.is_ge), inc=s_dve)
        p.emit("vector", lambda v: v.tensor_scalar(e_sb[:], e_sb[:], rden[:],
                                                   None, OP.mult), inc=s_dve)
        p.wait("vector", s_dve, s_dve.v)
        p.emit("vector", lambda v, tb=tb: v.tensor_tensor(
            comb[:, tb * 16:(tb + 1) * 16], e_sb[:],
            mask_all[:, tb * 16:(tb + 1) * 16], OP.mult), inc=s_dve)
    if debug:
        p.wait("gpsimd", s_dve, s_dve.v)
        p.emit("gpsimd", lambda g: g.dma_start(dbg_comb, comb[:]), inc=s_st, k=16)

    mask_done = s_dve.v
    for tb in range(NTB):
        p.wait("tensor", s_dve, mask_done)
        p.wait("tensor", s_dve, s_dve.v)
        for tc in range(tb + 1):
            p.emit("tensor", lambda t, tb=tb, tc=tc: t.matmul(
                ps_gt, tri_bf[:] if tc == tb else eyeones_bf[:, 128:256],
                mask_all[:, tc * 16:(tc + 1) * 16],
                start=(tc == 0), stop=(tc == tb)),
                inc=(s_pe if tc == tb else None))
        p.wait("vector", s_pe, s_pe.v)
        p.emit("vector", lambda v: v.tensor_scalar(tmp16a[:], ps_gt, 1.0,
                                                   None, OP.add), inc=s_dve)
        p.wait("vector", s_dve, s_dve.v)
        p.emit("vector", lambda v, tb=tb: v.tensor_tensor(
            tmp16b[:], tmp16a[:], mask_all[:, tb * 16:(tb + 1) * 16], OP.mult),
            inc=s_dve)
        p.wait("vector", s_dve, s_dve.v)
        p.emit("vector", lambda v, tb=tb: v.tensor_scalar(
            mposm1[:, tb * 20: tb * 20 + 16], tmp16b[:], -2.0, None, OP.add),
            inc=s_dve)
    mpos_done = s_dve.v
    p.wait("tensor", s_dve, mpos_done)
    for tb in range(NTB):
        par = tb % 2
        p.wait("tensor", s_dve, s_dve.v if tb >= 2 else 0)
        p.emit("tensor", lambda t, tb=tb, par=par: t.transpose(
            ps_pr[par][0:20, 0:128], mposm1[:, tb * 20:(tb + 1) * 20],
            eye_f32[:]), inc=s_pe)
        p.wait("vector", s_pe, s_pe.v)
        p.emit("vector", lambda v, tb=tb, par=par: v.tensor_copy(
            mT_sb[:, tb * 128:(tb + 1) * 128], ps_pr[par][0:20, 0:128]), inc=s_dve)

    a_pe_final = s_pe.v
    a_act_final = s_act.v
    a_dve_final = s_dve.v
    a_st_final = s_st.v
    es_a1.close()
    es_a.close()

    CAP = 512
    NSC = CAP // 128
    EIDX = list(range(16)) + [16, 16, 17, 17]
    MCOL = list(range(16)) + [16, 17, 18, 19]
    NPE = 20
    es_c = ExitStack()
    sbc = lambda n, s, d: es_c.enter_context(nc.sbuf_tensor(n, s, d))
    acc = sbc("acc", [128, NTB * DM], F32)
    es_moe = ExitStack()
    sbm = lambda n, s, d: es_moe.enter_context(nc.sbuf_tensor(n, s, d))
    xn_tm_sb = sbm("xn_tm_sb", [128, NTB * DM], BF16)
    w13r = [sbm(f"w13r{i}", [128, 4096], BF16) for i in range(2)]
    w2r = [sbm(f"w2r{i}", [128, NJ * 512], BF16) for i in range(2)]
    sel_sb = sbm("sel_sb", [128, NTB * CAP], BF16)
    selT_sb = sbm("selT_sb", [128, NSC * TOK], BF16)
    xng_sb = sbm("xng_sb", [128, NI * CAP], BF16)
    hg_sb = sbm("hg_sb", [128, NJ * CAP], BF16)
    yg_b = [sbm(f"yg_b{i}", [128, NSC * 512], BF16) for i in range(2)]
    sg_b = [sbm(f"sg_b{i}", [128, CAP], F32) for i in range(2)]
    mrow_sb = sbm("mrow_sb", [1, TOK], F32)
    ps_g = es_moe.enter_context(nc.psum_tensor("ps_g", [128, 1024], F32))
    ps_u = es_moe.enter_context(nc.psum_tensor("ps_u", [128, 1024], F32))
    ps_y = [es_moe.enter_context(nc.psum_tensor(f"ps_y{i}", [128, 512], F32))
            for i in range(2)]
    ps_sc = [es_moe.enter_context(nc.psum_tensor(f"ps_sc{i}", [128, 512], F32))
             for i in range(2)]
    sc_slots = [ps_sc[0][:], ps_sc[1][:], None, None]

    sc_slots[2] = ps_g[:, 0:512]
    sc_slots[3] = ps_g[:, 512:1024]
    fence("gpsimd", s_pe, s_act, s_dve, s_st)
    fence("vector", s_st)
    fence("sync", s_pe, s_act, s_dve)
    p.emit("sync", lambda s: s.dma_start(
        xn_tm_sb[:].rearrange("p (a c) -> p a c", c=DM),
        xn_tm_dram.rearrange("(a p) c -> p a c", p=128)), inc=s_ld, k=16)
    xn_tm_ready = s_ld.v

    w13_free = [0, 0]
    w2_free = [0, 0]
    w13_ready = {}
    w2_ready = {}
    sg_free = [0, 0]
    psg_free = 0
    ga_free = [0, 0]
    psu_free = [0, 0]
    posb_ps_free = 0
    psy_free = [0, 0]
    sc_free = [0, 0, 0, 0]
    acc_ready = {}
    sel_free = 0
    mrow_free = 0
    selT_free = 0
    xng_free = 0
    hg_free = 0
    yg_free = [0, 0]
    gj = 0
    gq = 0

    def dma_w13(l, j):
        slot = (l * NJ + j) % 2
        p.wait("gpsimd", s_pe, w13_free[slot])
        p.emit("gpsimd", lambda g, l=l, j=j, slot=slot: g.dma_start(
            w13r[slot][:], w13_img[EIDX[l], j]), inc=s_wd, k=16)
        w13_ready[(l, j)] = s_wd.v

    def dma_w2(l, q):
        slot = (l * 4 + q) % 2
        p.wait("gpsimd", s_pe, w2_free[slot])
        p.emit("gpsimd", lambda g, l=l, q=q, slot=slot: g.dma_start(
            w2r[slot][:], w2_img[EIDX[l], q]), inc=s_wd, k=16)
        w2_ready[(l, q)] = s_wd.v

    for k in range(2):
        dma_w13(0, k)
    dma_w2(0, 0)
    dma_w2(0, 1)

    def emit_sel(l):
        mc = MCOL[l]
        p.wait("vector", s_pe, sel_free)
        for tb in range(NTB):
            p.emit("vector", lambda v, tb=tb, mc=mc: v.tensor_scalar(
                sel_sb[:, tb * CAP:(tb + 1) * CAP], iotaS_sb[:, 0:CAP],
                mposm1[:, tb * 20 + mc: tb * 20 + mc + 1], None, OP.is_equal),
                inc=s_dve)
        return s_dve.v

    sel_done = emit_sel(0)
    for l in range(NPE):
        mc = MCOL[l]
        p.wait("gpsimd", s_pe, mrow_free)
        p.emit("gpsimd", lambda g, mc=mc: g.dma_start(
            mrow_sb[:], mT_sb[mc:mc + 1, :]), inc=s_wd, k=16)
        mrow_ready = s_wd.v
        p.wait("tensor", s_dve, max(psu_free))
        p.wait("tensor", s_dve, posb_ps_free)
        p.wait("tensor", s_wd, mrow_ready)
        for half in range(2):
            p.emit("tensor", lambda t, half=half: t.matmul(
                ps_u[:, half * 512:(half + 1) * 512],
                ones_f32r[:],
                mrow_sb[:, half * 512:(half + 1) * 512],
                start=True, stop=True), inc=(s_pe if half == 1 else None))
        mrow_free = s_pe.v
        p.wait("vector", s_pe, s_pe.v)
        p.wait("vector", s_pe, selT_free)
        for sc in range(NSC):
            p.emit("vector", lambda v, sc=sc: v.tensor_scalar(
                selT_sb[:, sc * TOK:(sc + 1) * TOK], ps_u[:],
                iotaCol_sb[:, sc:sc + 1], None, OP.is_equal), inc=s_dve)
        selT_done = s_dve.v
        posb_ps_free = s_dve.v
        p.wait("tensor", s_dve, sel_done)
        if l == 0:
            p.wait("tensor", s_ld, xn_tm_ready)
        p.wait("tensor", s_pe, xng_free)
        for dc in range(NI):
            par = dc % 2
            p.wait("tensor", s_act, max(ga_free[par], psg_free))
            p.wait("tensor", s_dve, sc_free[2 + par])
            for tb in range(NTB):
                p.emit("tensor", lambda t, dc=dc, tb=tb, par=par: t.matmul(
                    ps_g[:, par * 512:(par + 1) * 512],
                    xn_tm_sb[:, tb * DM + dc * 128: tb * DM + (dc + 1) * 128],
                    sel_sb[:, tb * CAP:(tb + 1) * CAP],
                    start=(tb == 0), stop=(tb == NTB - 1)),
                    inc=(s_pe if tb == NTB - 1 else None))
            p.wait("scalar", s_pe, s_pe.v)
            p.emit("scalar", lambda s, dc=dc, par=par: s.activation(
                xng_sb[:, dc * CAP:(dc + 1) * CAP], ps_g[:, par * 512:(par + 1) * 512],
                AF.Identity), inc=s_act)
            ga_free[par] = s_act.v
        sel_free = s_pe.v
        ga_done = s_act.v
        for j in range(NJ):
            slot = (l * NJ + j) % 2
            par = gj % 2
            p.wait("tensor", s_wd, w13_ready[(l, j)])
            p.wait("tensor", s_act, max(psg_free, ga_done))
            for ic in range(NI):
                p.emit("tensor", lambda tt, slot=slot, ic=ic: tt.matmul(
                    ps_g[:, 0:512],
                    w13r[slot][:, ic * 128:(ic + 1) * 128],
                    xng_sb[:, ic * CAP:(ic + 1) * CAP],
                    start=(ic == 0), stop=(ic == NI - 1)),
                    inc=(s_pe if ic == NI - 1 else None))
            pe_g_done = s_pe.v
            uh = j % 2
            p.wait("tensor", s_dve, psu_free[uh])
            p.wait("tensor", s_dve, posb_ps_free)
            for ic in range(NI):
                p.emit("tensor", lambda tt, slot=slot, ic=ic, uh=uh: tt.matmul(
                    ps_u[:, uh * 512:(uh + 1) * 512],
                    w13r[slot][:, 2048 + ic * 128: 2048 + (ic + 1) * 128],
                    xng_sb[:, ic * CAP:(ic + 1) * CAP],
                    start=(ic == 0), stop=(ic == NI - 1)),
                    inc=(s_pe if ic == NI - 1 else None))
            pe_u_done = s_pe.v
            w13_free[slot] = s_pe.v
            p.wait("scalar", s_pe, pe_g_done)
            p.wait("scalar", s_dve, sg_free[par])
            p.emit("scalar", lambda s, par=par: s.activation(
                sg_b[par][:], ps_g[:, 0:512], AF.Silu), inc=s_act)
            psg_free = s_act.v
            p.wait("vector", s_act, s_act.v)
            p.wait("vector", s_pe, pe_u_done)
            if j == 0:
                p.wait("vector", s_pe, hg_free)
            p.emit("vector", lambda v, j=j, par=par, uh=uh: v.tensor_tensor(
                hg_sb[:, j * CAP:(j + 1) * CAP], sg_b[par][:],
                ps_u[:, uh * 512:(uh + 1) * 512],
                OP.mult), inc=s_dve)
            psu_free[uh] = s_dve.v
            sg_free[par] = s_dve.v
            gj += 1
            nxt = l * NJ + j + 2
            if nxt < NPE * NJ:
                dma_w13(nxt // NJ, nxt % NJ)
        h_done = s_dve.v
        xng_free = s_pe.v
        if l + 1 < NPE:
            sel_done = emit_sel(l + 1)
        p.wait("tensor", s_dve, selT_done)
        yg_q_done = {}

        def emit_scatter(l, q):
            qpar = (l * 4 + q) % 2
            p.wait("tensor", s_act, yg_q_done[q])
            for tb in range(NTB):
                par = tb % 4
                slot = sc_slots[par]
                p.wait("tensor", s_dve, sc_free[par])
                if par >= 2:
                    p.wait("tensor", s_act, psg_free)
                for sc in range(NSC):
                    p.emit("tensor", lambda t, tb=tb, sc=sc, slot=slot, qpar=qpar: t.matmul(
                        slot,
                        selT_sb[:, sc * TOK + tb * 128: sc * TOK + tb * 128 + 128],
                        yg_b[qpar][:, sc * 512:(sc + 1) * 512],
                        start=(sc == 0), stop=(sc == NSC - 1)),
                        inc=(s_pe if sc == NSC - 1 else None))
                p.wait("vector", s_pe, s_pe.v)
                if (tb, q) in acc_ready:
                    p.wait("vector", s_dve, acc_ready[(tb, q)])
                accsl = lambda tb=tb, q=q: acc[:, tb * DM + q * 512: tb * DM + (q + 1) * 512]
                if l == 0:
                    p.emit("vector", lambda v, tb=tb, q=q, slot=slot, l=l, accsl=accsl:
                           v.tensor_scalar(accsl(), slot,
                                           comb[:, tb * 16 + l: tb * 16 + l + 1],
                                           None, OP.mult), inc=s_dve)
                elif l < 16:
                    p.emit("vector", lambda v, tb=tb, q=q, slot=slot, l=l, accsl=accsl:
                           v.scalar_tensor_tensor(accsl(), slot,
                                                  comb[:, tb * 16 + l: tb * 16 + l + 1],
                                                  accsl(), OP.mult, OP.add), inc=s_dve)
                else:
                    p.emit("vector", lambda v, tb=tb, q=q, slot=slot, accsl=accsl:
                           v.tensor_tensor(accsl(), slot, accsl(), OP.add),
                           inc=s_dve)
                sc_free[par] = s_dve.v
                acc_ready[(tb, q)] = s_dve.v
            return s_pe.v

        for q in range(4):
            slot = (l * 4 + q) % 2
            qpar = (l * 4 + q) % 2
            p.wait("tensor", s_wd, w2_ready[(l, q)])
            p.wait("tensor", s_pe, yg_free[qpar])
            for sc in range(NSC):
                par = gq % 2
                p.wait("tensor", s_act, psy_free[par])
                if sc == 0 and q == 0:
                    p.wait("tensor", s_dve, h_done)
                for ic in range(NJ):
                    p.emit("tensor", lambda tt, slot=slot, ic=ic, sc=sc, par=par:
                           tt.matmul(ps_y[par][:],
                                     hg_sb[:, ic * CAP + sc * 128: ic * CAP + sc * 128 + 128],
                                     w2r[slot][:, ic * 512:(ic + 1) * 512],
                                     start=(ic == 0), stop=(ic == NJ - 1)),
                           inc=(s_pe if ic == NJ - 1 else None))
                p.wait("scalar", s_pe, s_pe.v)
                p.emit("scalar", lambda s, qpar=qpar, sc=sc, par=par: s.activation(
                    yg_b[qpar][:, sc * 512:(sc + 1) * 512],
                    ps_y[par][:], AF.Identity), inc=s_act)
                psy_free[par] = s_act.v
                gq += 1
            w2_free[slot] = s_pe.v
            yg_q_done[q] = s_act.v
            nxtq = l * 4 + q + 2
            if nxtq < NPE * 4:
                dma_w2(nxtq // 4, nxtq % 4)
            if q >= 1:
                yg_free[(l * 4 + q - 1) % 2] = emit_scatter(l, q - 1)
        yg_free[(l * 4 + 3) % 2] = emit_scatter(l, 3)
        hg_free = s_pe.v
        selT_free = s_pe.v

    c_pe_final = s_pe.v
    c_dve_final = s_dve.v
    c_act_final = s_act.v
    if debug:
        p.wait("gpsimd", s_dve, s_dve.v)
        p.emit("gpsimd", lambda g: g.dma_start(dbg_accC, acc[:]), inc=s_st, k=16)
    es_moe.close()

    es_b = ExitStack()
    sbb = lambda n, s, d: es_b.enter_context(nc.sbuf_tensor(n, s, d))
    o_all = sbb("o_all", [128, NH * TOK], BF16)
    wo_sb = [sbb(f"wo_sb{i}", [128, NH * 512], BF16) for i in range(2)]
    es_b1 = ExitStack()
    sbb1 = lambda n, s, d: es_b1.enter_context(nc.sbuf_tensor(n, s, d))
    k_sb = [sbb1(f"k_sb{i}", [128, SEQ], BF16) for i in range(2)]
    v_sb = [sbb1(f"v_sb{i}", [128, SEQ], BF16) for i in range(2)]
    q_sb = [sbb1(f"q_sb{i}", [128, TOK], BF16) for i in range(2)]
    p_sb = [sbb1(f"p_sb{i}", [128, 1024], BF16) for i in range(3)]
    rb_sb = [sbb1(f"rb_sb{i}", [128, 512], F32) for i in range(2)]
    ps_s = [es_b1.enter_context(nc.psum_tensor(f"ps_s{i}", [128, 1024], F32))
            for i in range(2)]
    ps_oz = [es_b1.enter_context(nc.psum_tensor(f"ps_oz{i}", [128, 1024], F32))
             for i in range(2)]

    fence("sync", s_pe, s_dve, s_act)
    fence("gpsimd", s_pe, s_dve)

    kvq_ready = {}
    kvq_free = [0, 0]

    def dma_head(hh):
        par = hh % 2
        p.wait("sync", s_pe, kvq_free[par])
        p.emit("sync", lambda s, hh=hh, par=par: s.dma_start(k_sb[par][:], k_dram[hh]),
               inc=s_ld, k=16)
        p.emit("sync", lambda s, hh=hh, par=par: s.dma_start(v_sb[par][:], v_dram[hh]),
               inc=s_ld, k=16)
        p.emit("sync", lambda s, hh=hh, par=par: s.dma_start(q_sb[par][:], q_dram[hh]),
               inc=s_ld, k=16)
        kvq_ready[hh] = s_ld.v

    dma_head(0)
    dma_head(1)
    wo_ready = {}
    wo_free = [0, 0]

    def dma_wo(q):
        slot = q % 2
        p.wait("gpsimd", s_pe, wo_free[slot])
        p.emit("gpsimd", lambda g, q=q, slot=slot: g.dma_start(
            wo_sb[slot][:], wo_img[q]), inc=s_wd, k=16)
        wo_ready[q] = s_wd.v

    dma_wo(0)
    dma_wo(1)

    pss_free = [0, 0]
    pso_free = 0
    psz_free = 0
    p_free = [0, 0, 0]
    NKC = SEQ // 128
    NPC = NKC // 2

    def emit_pv_pair(par, pc, opar):
        pr = pc % 3
        for half in range(2):
            kc = 2 * pc + half
            p.emit("tensor", lambda t, par=par, kc=kc, pr=pr, half=half, opar=opar:
                   t.matmul(ps_oz[opar][:, 0:512],
                            v_sb[par][:, kc * 128:(kc + 1) * 128],
                            p_sb[pr][:, half * 512:(half + 1) * 512],
                            start=(kc == 0), stop=(kc == NKC - 1)))
            p.emit("tensor", lambda t, kc=kc, pr=pr, half=half, opar=opar: t.matmul(
                ps_oz[opar][:, 512:1024], eyeones_bf[:, 128:256],
                p_sb[pr][:, half * 512:(half + 1) * 512],
                start=(kc == 0), stop=(kc == NKC - 1)),
                inc=(s_pe if half == 1 else None))
        p_free[pr] = s_pe.v

    pso_free2 = [0, 0]
    for hh in range(NH):
        par = hh % 2
        p.wait("tensor", s_ld, kvq_ready[hh])
        for qt in range(2):
            opar = (hh * 2 + qt) % 2
            p.wait("tensor", s_dve, pso_free2[opar])
            exp_done = {}
            for pc in range(NPC):
                sp = pc % 2
                pr = pc % 3
                p.wait("tensor", s_act, pss_free[sp])
                for half in range(2):
                    kc = 2 * pc + half
                    p.emit("tensor", lambda t, par=par, qt=qt, kc=kc, sp=sp, half=half:
                           t.matmul(ps_s[sp][:, half * 512:(half + 1) * 512],
                                    k_sb[par][:, kc * 128:(kc + 1) * 128],
                                    q_sb[par][:, qt * 512:(qt + 1) * 512],
                                    start=True, stop=True),
                           inc=(s_pe if half == 1 else None))
                p.wait("scalar", s_pe, s_pe.v)
                p.wait("scalar", s_pe, p_free[pr])
                p.emit("scalar", lambda s, sp=sp, pr=pr: s.activation(
                    p_sb[pr][:], ps_s[sp][:], AF.Exp, bias=c_neg12[:]), inc=s_act)
                pss_free[sp] = s_act.v
                exp_done[pc] = s_act.v
                if pc >= 2:
                    p.wait("tensor", s_act, exp_done[pc - 2])
                    emit_pv_pair(par, pc - 2, opar)
            for tail in (NPC - 2, NPC - 1):
                p.wait("tensor", s_act, exp_done[tail])
                emit_pv_pair(par, tail, opar)
            pv_done = s_pe.v
            p.wait("vector", s_pe, pv_done)
            p.emit("vector", lambda v, opar=opar: v.reciprocal(
                rb_sb[opar][:], ps_oz[opar][:, 512:1024]), inc=s_dve)
            p.wait("vector", s_dve, s_dve.v)
            p.emit("vector", lambda v, hh=hh, qt=qt, opar=opar: v.tensor_tensor(
                o_all[:, hh * TOK + qt * 512: hh * TOK + (qt + 1) * 512],
                ps_oz[opar][:, 0:512], rb_sb[opar][:], OP.mult), inc=s_dve)
            pso_free2[opar] = s_dve.v
        kvq_free[par] = s_pe.v
        if hh + 2 < NH:
            dma_head(hh + 2)

    b1_pe = s_pe.v
    b1_dve = s_dve.v
    if debug:
        p.wait("gpsimd", s_dve, s_dve.v)
        p.emit("gpsimd", lambda g: g.dma_start(dbg_oall, o_all[:]), inc=s_st, k=16)
    es_b1.close()

    es_b2 = ExitStack()
    ps_w = [es_b2.enter_context(nc.psum_tensor(f"ps_w{i}", [128, 512], F32))
            for i in range(2)]
    psw_free = [0, 0]
    p.wait("tensor", s_dve, b1_dve)
    p.wait("tensor", s_act, s_act.v)
    for q in range(4):
        slot = q % 2
        p.wait("tensor", s_wd, wo_ready[q])
        for tb in range(NTB):
            par = (q * NTB + tb) % 2
            p.wait("tensor", s_dve, psw_free[par])
            for hh in range(NH):
                p.emit("tensor", lambda t, q=q, slot=slot, tb=tb, hh=hh, par=par:
                       t.matmul(ps_w[par][:, 0:512],
                                o_all[:, hh * TOK + tb * 128: hh * TOK + tb * 128 + 128],
                                wo_sb[slot][:, hh * 512:(hh + 1) * 512],
                                start=(hh == 0), stop=(hh == NH - 1)),
                       inc=(s_pe if hh == NH - 1 else None))
            p.wait("vector", s_pe, s_pe.v)
            p.wait("vector", s_dve, acc_ready[(tb, q)])
            p.emit("vector", lambda v, q=q, tb=tb, par=par: v.scalar_tensor_tensor(
                acc[:, tb * DM + q * 512: tb * DM + (q + 1) * 512],
                ps_w[par][:, 0:512], 1.0,
                acc[:, tb * DM + q * 512: tb * DM + (q + 1) * 512],
                OP.mult, OP.add), inc=s_dve)
            psw_free[par] = s_dve.v
            acc_ready[(tb, q)] = s_dve.v
        wo_free[slot] = s_pe.v
        if q + 2 < 4:
            dma_wo(q + 2)
    es_b2.close()
    es_b.close()

    es_d = ExitStack()
    xo_t = [es_d.enter_context(nc.sbuf_tensor(f"xo_t{i}", [128, DM], F32))
            for i in range(2)]
    o_t = [es_d.enter_context(nc.sbuf_tensor(f"o_t{i}", [128, DM], F32))
           for i in range(2)]
    fence("sync", s_pe, s_dve)
    ot_free = [0, 0]
    xo_free = [0, 0]
    for tb in range(NTB):
        par = tb % 2
        p.wait("sync", s_dve, xo_free[par])
        p.emit("sync", lambda s, tb=tb, par=par: s.dma_start(
            xo_t[par][:], xb[tb * 128:(tb + 1) * 128, :]), inc=s_ld, k=16)
        p.wait("vector", s_ld, s_ld.v)
        p.wait("vector", s_dve, max(acc_ready[(tb, q)] for q in range(4)))
        p.wait("vector", s_st, ot_free[par])
        p.emit("vector", lambda v, tb=tb, par=par: v.tensor_tensor(
            o_t[par][:], acc[:, tb * DM:(tb + 1) * DM], xo_t[par][:], OP.add),
            inc=s_dve)
        xo_free[par] = s_dve.v
        p.wait("sync", s_dve, s_dve.v)
        p.emit("sync", lambda s, tb=tb, par=par: s.dma_start(
            out[tb * 128:(tb + 1) * 128, :], o_t[par][:]), inc=s_st, k=16)
        ot_free[par] = s_st.v
    p.wait("sync", s_st, s_st.v)
    p.wait("gpsimd", s_st, s_st.v)

    es_d.close()
    es_c.close()

    with nc.Block() as blk:
        p.into(blk)
    es_g.close()
    n_inst = sum(len(v) for v in p.ops.values())
    return nc, n_inst


def _images(L0_attn_wq, L0_attn_wk, L0_attn_wv, L0_attn_wo,
            L0_ffn_gate_w, L0_ffn_w13, L0_ffn_w2, L0_ffn_sw13, L0_ffn_sw2):
    f32 = np.float32
    wq = np.asarray(L0_attn_wq, f32)
    wk = np.asarray(L0_attn_wk, f32)
    wv = np.asarray(L0_attn_wv, f32)
    wo = np.asarray(L0_attn_wo, f32)
    gw = np.asarray(L0_ffn_gate_w, f32)
    w13 = np.asarray(L0_ffn_w13, f32)
    w2 = np.asarray(L0_ffn_w2, f32)
    sw13 = np.asarray(L0_ffn_sw13, f32)
    sw2 = np.asarray(L0_ffn_sw2, f32)

    def colblocks(w):
        return np.ascontiguousarray(
            w.reshape(NI, 128, NI, 128).transpose(0, 3, 2, 1).reshape(NI, 128, DM))
    wq_img = colblocks(wq)
    wk_img = colblocks(wk)
    wvq_img = np.ascontiguousarray(
        wv.reshape(4, 512, NI, 128).transpose(0, 3, 2, 1).reshape(4, 128, NI * 512))
    wo_img = np.ascontiguousarray(
        wo.reshape(4, 512, NH, 128).transpose(0, 3, 2, 1).reshape(4, 128, NH * 512))
    gw_img = np.ascontiguousarray(
        gw.T.reshape(NI, 128, 16).transpose(1, 0, 2).reshape(128, NI * 16))
    w13v = w13.reshape(16, 2, NJ, 128, NI, 128)
    w13_r = w13v.transpose(0, 2, 5, 1, 4, 3).reshape(16, NJ, 128, 4096)
    sw13v = sw13.reshape(2, 2, NJ, 128, NI, 128)
    sw13_r = sw13v.transpose(1, 2, 5, 0, 4, 3).reshape(2, NJ, 128, 4096)
    w13_img = np.ascontiguousarray(np.concatenate([w13_r, sw13_r], axis=0))
    w2v = w2.reshape(16, 4, 512, NJ, 128)
    w2_r = w2v.transpose(0, 1, 4, 3, 2).reshape(16, 4, 128, NJ * 512)
    sw2v = sw2.reshape(4, 512, 2, NJ, 128)
    sw2_r = sw2v.transpose(2, 0, 4, 3, 1).reshape(2, 4, 128, NJ * 512)
    w2_img = np.ascontiguousarray(np.concatenate([w2_r, sw2_r], axis=0))

    eyeones = np.concatenate(
        [np.eye(128, dtype=f32), np.ones((128, 128), f32)], axis=1)
    iotaS = np.broadcast_to(np.arange(512, dtype=f32)[None, :], (128, 512))
    pcol = np.arange(128, dtype=f32)[:, None]
    iotaCol = pcol + 512.0 * 0 + np.arange(4, dtype=f32)[None, :] * 128.0
    tri = (np.arange(128)[:, None] <= np.arange(128)[None, :]).astype(f32)
    mconst = np.full((128, 8, 4), -2.0, f32)
    for tb in range(8):
        for q in range(4):
            first = (q % 2 == 0)
            if first and tb < 4:
                mconst[:, tb, q] = tb * 128 + pcol[:, 0]
            if (not first) and tb >= 4:
                mconst[:, tb, q] = (tb - 4) * 128 + pcol[:, 0]
    consts2 = np.ascontiguousarray(np.concatenate(
        [iotaS, iotaCol, tri, mconst.reshape(128, 32)], axis=1))

    return {"wq_img": wq_img, "wk_img": wk_img, "wvq_img": wvq_img,
            "wo_img": wo_img, "gw_img": gw_img, "w13_img": w13_img,
            "w2_img": w2_img, "eyeones": eyeones, "consts2": consts2}


def kernel(L0_attn_x, L0_attn_wq, L0_attn_wk, L0_attn_wv, L0_attn_wo,
           L0_ffn_gate_w, L0_ffn_w13, L0_ffn_w2, L0_ffn_sw13, L0_ffn_sw2):
    x = np.asarray(L0_attn_x, np.float32)
    imgs = _images(L0_attn_wq, L0_attn_wk, L0_attn_wv, L0_attn_wo,
                   L0_ffn_gate_w, L0_ffn_w13, L0_ffn_w2, L0_ffn_sw13, L0_ffn_sw2)
    nc, _ = build_program()
    in_maps = []
    for c in range(NCORES):
        b, qs = c // 4, c % 4
        xbat = x[b]
        own = xbat[qs * TOK:(qs + 1) * TOK]
        rest = np.concatenate([xbat[:qs * TOK], xbat[(qs + 1) * TOK:]], axis=0)
        xb_c = np.ascontiguousarray(np.concatenate([own, rest], axis=0))
        in_maps.append({"xb": xb_c, **imgs})
    res = run_bass_kernel_spmd(nc, in_maps, core_ids=list(range(NCORES)))
    outp = np.empty((BATCH, SEQ, DM), np.float32)
    for c in range(NCORES):
        b, qs = c // 4, c % 4
        outp[b, qs * TOK:(qs + 1) * TOK] = res.results[c]["out"]
    return outp
